# version 01
# baseline (speedup 1.0000x reference)
import numpy as np
from contextlib import ExitStack
import concourse.bass as bass
import concourse.mybir as mybir
from concourse.bass_utils import run_bass_kernel_spmd

F32 = mybir.dt.float32
BF16 = mybir.dt.bfloat16
AF = mybir.ActivationFunctionType
ALU = mybir.AluOpType
AX = mybir.AxisListType

B, T, D = 4, 4096, 1024
HALO, TOWN = 128, 2048
TEXT = TOWN + HALO
NCH = TEXT // 128
NE = 66
EI = 704
NH, HD = 8, 128
BIG = 1.0e9
SLOT0 = 1056
ICS = [128, 128, 128, 128, 128, 64]
TSL = [(0, 512), (512, 512), (1024, 512), (1536, 512), (2048, 128)]


def build_program(dbg=(), skip=()):
    nc = bass.Bass("TRN2", target_bir_lowering=False,
                   detect_race_conditions=False)
    dbg = set(dbg)
    skip = set(skip)

    def dram_in(name, shape, dt=F32):
        return nc.dram_tensor(name, shape, dt, kind="ExternalInput").ap()

    dx = dram_in("x", [TEXT, D])
    dwqkv = dram_in("wqkv", [D, 3072])
    dwout = dram_in("wout", [128, 8 * 1024])
    dgamma = dram_in("gamma_col", [128, 8])
    dtemp = dram_in("temp_col", [128, 8])
    dgw = dram_in("gate_wT", [D, 64])
    dgb = dram_in("gate_bias_b", [128, 64])
    dWg = dram_in("Wg_all", [NE, 128, 8 * EI])
    dWu = dram_in("Wu_all", [NE, 128, 8 * EI])
    dWd = dram_in("Wd_all", [NE, 128, 6 * 1024])
    dencq = dram_in("encq", [128, 8 * 1024])
    dencv = dram_in("encv", [128, 8 * 1024])
    dw1 = dram_in("mw1", [128, 4096])
    db1 = dram_in("mb1_col", [128, 4])
    dw2 = dram_in("mw2", [128, 4096])
    db2 = dram_in("mb2_col", [128, 8])
    dbase = dram_in("base_col", [128, 8])
    dhU = dram_in("hebbUT", [32, 1024])
    dhV = dram_in("hebbV", [32, 1024])
    ddec = dram_in("decoder", [128, 8192])
    dwfcT = dram_in("w_fcT", [D, 4096])
    dwprT = dram_in("w_projT", [4096, D])
    duT = dram_in("uT", [16, 4096])
    dvlr = dram_in("vT_lr", [128, 128])
    dfbT = dram_in("fbT", [16, 16])
    dgate = dram_in("gate_col", [16, 1])
    dind = dram_in("ind_col", [128, 4])
    didb = dram_in("ident_bf", [128, 128], BF16)
    didf = dram_in("ident_f32", [128, 128])
    dones = dram_in("ones_f32_sq", [128, 128])
    donesc = dram_in("ones_f32_col", [128, 1])
    dmaskm = dram_in("mask_mult", [128, 256])
    dmaska = dram_in("mask_add", [128, 256])
    depsc = dram_in("eps_col", [128, 8])
    dselr = dram_in("selrow", [1, 256])

    dy = nc.dram_tensor("y", [TOWN, D], F32, kind="ExternalOutput").ap()
    ddbg = {}
    for name, shape in [("dbg_a", [128, 8 * TEXT]), ("dbg_b", [128, 8 * TEXT]),
                        ("dbg_c", [128, 4096])]:
        if name in dbg:
            ddbg[name] = nc.dram_tensor(name, shape, F32,
                                        kind="ExternalOutput").ap()

    dvspill = nc.dram_tensor("vspill", [128, 8 * TEXT], BF16, kind="Internal").ap()
    dqspill = nc.dram_tensor("qspill", [NH, 128, TEXT], F32, kind="Internal").ap()
    dskew = [nc.dram_tensor(f"skew{i}", [128, 8, 257], F32, kind="Internal").ap()
             for i in range(2)]
    dcin0 = nc.dram_tensor("cin0", [128, 4 * SLOT0], F32, kind="Internal").ap()
    dcout0 = nc.dram_tensor("cout0", [128, 4 * SLOT0], F32, kind="Internal",
                            addr_space="Shared").ap()
    dcin1 = nc.dram_tensor("cin1", [128, 64], F32, kind="Internal").ap()
    dcout1 = nc.dram_tensor("cout1", [128, 64], F32, kind="Internal",
                            addr_space="Shared").ap()

    top = ExitStack()
    mk_sb = lambda n, s, d: nc.sbuf_tensor(n, s, d)
    mk_ps = lambda n, s: nc.psum_tensor(n, s, F32)

    def alloc(stack, *specs):
        out = []
        for sp in specs:
            if sp[0] == "ps":
                out.append(stack.enter_context(mk_ps(sp[1], sp[2])))
            else:
                out.append(stack.enter_context(mk_sb(sp[1], sp[2], sp[3])))
        return out

    with top:
        xnT = top.enter_context(mk_sb("xnT", [128, 8, TEXT], BF16))
        gammac, tempc, gbias, indc, identb, identf, onessq, onescol = alloc(
            top,
            ("sb", "gammac", [128, 8], F32), ("sb", "tempc", [128, 8], F32),
            ("sb", "gbias", [128, 64], F32), ("sb", "indc", [128, 4], F32),
            ("sb", "identb", [128, 128], BF16),
            ("sb", "identf", [128, 128], F32),
            ("sb", "onessq", [128, 128], F32),
            ("sb", "onescol", [128, 1], F32))
        rwTb = top.enter_context(mk_sb("rwTb", [128, TEXT], BF16))
        epsc = top.enter_context(mk_sb("epsc", [128, 8], F32))
        gwTs = top.enter_context(mk_sb("gwTs", [128, 8, 64], BF16))

        sems = {}
        for sname in ["s_x", "s_w", "s_sp", "s_pe", "s_ve", "s_sc", "s_gp",
                      "s_cc", "s_out", "s_bar", "s_ld0", "s_ld1", "s_pk",
                      "s_vsp", "s_wa", "s_wb", "s_st0", "s_st1", "s_qsp",
                      "s_sk0", "s_sk1", "s_ex0", "s_ex1"]:
            sems[sname] = top.enter_context(nc.semaphore(sname))
        s_bar = sems["s_bar"]
        s_out = sems["s_out"]
        s_cc = sems["s_cc"]

        cnt = {}

        def inc(name, k=1):
            cnt[name] = cnt.get(name, 0) + k
            return cnt[name]

        def val(name):
            return cnt.get(name, 0)

        nbar = [0]

        with nc.Block() as blk:
            ENG = {"pe": blk.tensor, "ve": blk.vector, "sc": blk.scalar,
                   "sy": blk.sync, "gp": blk.gpsimd}
            ESEM = {"pe": "s_pe", "ve": "s_ve", "sc": "s_sc", "sy": "s_x",
                    "gp": "s_gp"}

            def step(eng, fn, waits=(), tag=None, dinc=None):
                dec = ENG[eng]
                import inspect as _insp
                has_f = "F" in _insp.signature(fn).parameters

                @dec
                def _(e):
                    for wtag in waits:
                        e.wait_ge(sems[cnt["@" + wtag]], cnt[wtag])
                    if has_f:
                        sname0 = ESEM[eng]

                        def F(instr):
                            instr.then_inc(sems[sname0], 1)
                            e.wait_ge(sems[sname0], inc(sname0))
                        r = fn(e, F)
                    else:
                        r = fn(e)
                    if dinc is None:
                        sname = ESEM[eng]
                        if r is None or r is True:
                            e.nop().then_inc(sems[sname], 1)
                        else:
                            r.then_inc(sems[sname], 1)
                        inc(sname, 1)
                    else:
                        sname, k = dinc
                        inc(sname, k)
                    if tag is not None:
                        cnt[tag] = val(sname)
                        cnt["@" + tag] = sname

            def dma(eng, dst, src, tag=None, waits=(), semname=None,
                    slow=False):
                if semname is None:
                    semname = {"sy": "s_x", "gp": "s_w", "sc": "s_sp"}[eng]

                def fn(e):
                    kw = {"allow_slow_non_contiguous": True} if slow else {}
                    e.dma_start(out=dst, in_=src, **kw
                                ).then_inc(sems[semname], 16)
                step(eng, fn, waits=waits, tag=tag, dinc=(semname, 16))

            def barrier():
                nbar[0] += 1
                tgt = 5 * nbar[0]
                for dec in (blk.tensor, blk.vector, blk.scalar, blk.sync,
                            blk.gpsimd):
                    @dec
                    def _(e):
                        e.nop().then_inc(s_bar, 1)
                        e.wait_ge(s_bar, tgt)

            env = dict(
                nc=nc, blk=blk, step=step, dma=dma, barrier=barrier, inc=inc,
                val=val, cnt=cnt, sems=sems, alloc=alloc, mk_sb=mk_sb,
                mk_ps=mk_ps, xnT=xnT, gwTs=gwTs, gammac=gammac,
                tempc=tempc,
                gbias=gbias, indc=indc, identb=identb, identf=identf,
                onessq=onessq, onescol=onescol, rwTb=rwTb,
                dx=dx, dwqkv=dwqkv, dwout=dwout, dgamma=dgamma, dtemp=dtemp,
                dgw=dgw, dgb=dgb, dWg=dWg, dWu=dWu, dWd=dWd, dencq=dencq,
                dencv=dencv, dw1=dw1, db1=db1, dw2=dw2, db2=db2, dbase=dbase,
                dhU=dhU, dhV=dhV, ddec=ddec, dwfcT=dwfcT, dwprT=dwprT,
                duT=duT, dvlr=dvlr, dfbT=dfbT, dgate=dgate, dind=dind,
                dy=dy, ddbg=ddbg, dvspill=dvspill, dqspill=dqspill,
                dskew=dskew, dcin0=dcin0, dcout0=dcout0, dcin1=dcin1,
                dcout1=dcout1, dmaskm=dmaskm, dmaska=dmaska, s_cc=s_cc,
                didb=didb, didf=didf, dones=dones, donesc=donesc,
                depsc=depsc, epsc=epsc, skip=skip, dselr=dselr,
            )
            _emit(env)
    return nc


def _emit(env):
    g = type("G", (), env)
    step, dma, barrier = env["step"], env["dma"], env["barrier"]
    alloc, mk_sb, mk_ps = env["alloc"], env["mk_sb"], env["mk_ps"]
    cnt, inc, val = env["cnt"], env["inc"], env["val"]
    blk, sems = env["blk"], env["sems"]

    res = {}

    def use(*names):
        return [res[n] for n in names if n in res]

    def done(name, tag):
        res[name] = tag

    with ExitStack() as pa:
        (wbig, xtok, xnck, xn2ck, xn2tT, vtmp, vtT, junk, stat, qkb, qksq,
         tmpw, pack, pack1) = alloc(
            pa,
            ("sb", "wbig", [128, 8, 3072], BF16),
            ("sb", "xtok", [128, 2, 1024], F32),
            ("sb", "xnck", [128, 2, 1024], BF16),
            ("sb", "xn2ck", [128, 2, 1024], BF16),
            ("sb", "xn2tT", [128, 8, 128], BF16),
            ("sb", "vtmp", [128, 1024], BF16),
            ("sb", "vtT", [128, 8, 128], BF16),
            ("sb", "junk", [128, 2, 1024], BF16),
            ("sb", "stat", [128, 2, 16], F32),
            ("sb", "qkb", [128, 2, 2048], BF16),
            ("sb", "qksq", [128, 2048], F32),
            ("sb", "tmpw", [128, 1024], F32),
            ("sb", "pack", [128, SLOT0], F32),
            ("sb", "pack1", [128, SLOT0], F32))
        psT0, psT1, psT2, psT3, psB, psC, psF, psG = alloc(
            pa,
            ("ps", "psT0", [128, 512]), ("ps", "psT1", [128, 512]),
            ("ps", "psT2", [128, 512]), ("ps", "psT3", [128, 512]),
            ("ps", "psB", [128, 512]), ("ps", "psC", [128, 512]),
            ("ps", "psF", [64, 8, 64]), ("ps", "psG", [64, 8, 64]))
        psT = [psT0, psT1, psT2, psT3]

        for dst, src in [(g.gammac, g.dgamma), (g.tempc, g.dtemp),
                         (g.gbias, g.dgb), (g.indc, g.dind),
                         (g.identb, g.didb), (g.identf, g.didf),
                         (g.onessq, g.dones), (g.onescol, g.donesc),
                         (g.epsc, g.depsc)]:
            dma("sy", dst[:], src[:])
        cnt["consts"] = val("s_x")
        cnt["@consts"] = "s_x"

        for c in range(8):
            dma("gp", wbig[:, c], g.dwqkv[c * 128:(c + 1) * 128, :])
        cnt["w_qkv"] = val("s_w")
        cnt["@w_qkv"] = "s_w"
        for c in range(8):
            dma("gp", g.gwTs[:, c], g.dgw[c * 128:(c + 1) * 128, :])
        cnt["w_gate"] = val("s_w")
        cnt["@w_gate"] = "s_w"

        def fold_gamma(e):
            for c in range(8):
                e.tensor_scalar(wbig[:, c], wbig[:, c],
                                g.gammac[:, c:c + 1], None, op0=ALU.mult)
        step("ve", fold_gamma, waits=["consts", "w_gate"], tag="ve_gamma")

        for i in range(NCH):
            stc = stat[:, i % 2]
            dma("sy", xtok[:, i % 2], g.dx[i * 128:(i + 1) * 128, :],
                tag=f"x_ld_{i}", waits=use(f"xtok{i % 2}"),
                semname=f"s_ld{i % 2}")

            if "ln" in g.skip:
                continue
            def f_sum(e, i=i, stc=stc):
                e.activation(junk[:, 0], xtok[:, i % 2], AF.Copy,
                             accum_out=stc[:, 0:1])
                return e.activation(junk[:, 1], xtok[:, i % 2], AF.Square,
                                    accum_out=stc[:, 1:2])
            wsum = [f"x_ld_{i}"] + use(f"stat{i % 2}")
            step("sc", f_sum, waits=wsum, tag=f"sc_sum_{i}")

            def f_var(e, F, stc=stc):
                e.tensor_scalar(stc[:, 6:7], stc[:, 0:1], 1.0 / 1024.0, None,
                                op0=ALU.mult)
                F(e.tensor_scalar(stc[:, 8:9], stc[:, 1:2], 1.0 / 1024.0,
                                  None, op0=ALU.mult))
                F(e.tensor_tensor(stc[:, 9:10], stc[:, 6:7], stc[:, 6:7],
                                  op=ALU.mult))
                return e.tensor_tensor(stc[:, 7:8], stc[:, 8:9], stc[:, 9:10],
                                       op=ALU.subtract)
            step("ve", f_var, waits=[f"sc_sum_{i}"], tag=f"ve_var_{i}")

            def f_sqrt(e, stc=stc):
                e.activation(stc[:, 10:11], stc[:, 7:8], AF.Sqrt,
                             bias=g.epsc[:, 0:1])
                return e.activation(stc[:, 11:12], stc[:, 7:8], AF.Sqrt,
                                    bias=0.0)
            step("sc", f_sqrt, waits=[f"ve_var_{i}"], tag=f"sc_sqrt_{i}")

            def f_stat(e, F, stc=stc):
                F(e.reciprocal(stc[:, 2:3], stc[:, 10:11]))
                F(e.tensor_tensor(stc[:, 12:13], stc[:, 11:12], stc[:, 2:3],
                                  op=ALU.mult))
                F(e.tensor_scalar(stc[:, 12:13], stc[:, 12:13], 32.0, None,
                                  op0=ALU.mult))
                F(e.tensor_scalar(stc[:, 12:13], stc[:, 12:13], 1e-12, None,
                                  op0=ALU.max))
                F(e.reciprocal(stc[:, 13:14], stc[:, 12:13]))
                F(e.tensor_scalar(stc[:, 13:14], stc[:, 13:14], 32.0, None,
                                  op0=ALU.mult))
                return e.tensor_tensor(stc[:, 4:5], stc[:, 2:3],
                                       stc[:, 13:14], op=ALU.mult)
            step("ve", f_stat, waits=[f"sc_sqrt_{i}"], tag=f"ve_stat_{i}")

            def f_norm(e, i=i, stc=stc):
                e.tensor_scalar(xnck[:, i % 2], xtok[:, i % 2],
                                stc[:, 6:7], stc[:, 2:3],
                                op0=ALU.subtract, op1=ALU.mult)
                return e.tensor_scalar(xn2ck[:, i % 2], xtok[:, i % 2],
                                       stc[:, 6:7], stc[:, 4:5],
                                       op0=ALU.subtract, op1=ALU.mult)
            step("ve", f_norm,
                 waits=[f"ve_stat_{i}"] + use(f"xnck{i % 2}",
                                              f"xn2ck{i % 2}"),
                 tag=f"sc_xn_{i}")
            done(f"xtok{i % 2}", f"sc_xn_{i}")
            done(f"stat{i % 2}", f"sc_xn_{i}")

            if "trn" in g.skip:
                continue
            for which, src in ((0, xnck), (1, xn2ck)):
                for dc in range(8):
                    k = which * 8 + dc
                    slot = f"psT{k % 4}"
                    wts = [f"sc_xn_{i}"] + use(slot)
                    if which == 1 and dc == 0:
                        wts += use("xn2tT")
                    step("pe",
                         lambda e, i=i, dc=dc, k=k, src=src: e.matmul(
                             psT[k % 4][:, 0:128],
                             src[:, i % 2, dc * 128:(dc + 1) * 128],
                             g.identb[:], start=True, stop=True),
                         waits=wts, tag=f"pe_trn_{i}_{k}")

                    def f_ev(e, i=i, dc=dc, k=k, which=which):
                        dst = (g.xnT[:, dc, i * 128:(i + 1) * 128]
                               if which == 0 else xn2tT[:, dc])
                        return e.tensor_copy(dst, psT[k % 4][:, 0:128])
                    step("ve", f_ev, waits=[f"pe_trn_{i}_{k}"],
                         tag=f"ve_trn_{i}_{k}")
                    done(slot, f"ve_trn_{i}_{k}")
            done(f"xnck{i % 2}", f"ve_trn_{i}_7")
            done(f"xn2ck{i % 2}", f"ve_trn_{i}_15")

            if "qkv" in g.skip:
                continue
            s6list = list(range(6)) if i >= 1 else [4, 5]
            for s6 in s6list:
                pq, pqn = (psB, "psB") if s6 % 2 == 0 else (psC, "psC")
                wts = [f"ve_trn_{i}_15"] + use(pqn)
                if i == 0 and s6 == s6list[0]:
                    wts.append("ve_gamma")

                def f_qkv(e, i=i, s6=s6, pq=pq):
                    r = None
                    for dc in range(8):
                        r = e.matmul(pq[:, 0:512], xn2tT[:, dc],
                                     wbig[:, dc, s6 * 512:(s6 + 1) * 512],
                                     start=(dc == 0), stop=(dc == 7))
                    return r
                step("pe", f_qkv, waits=wts, tag=f"pe_qk_{i}_{s6}")
                if s6 == s6list[-1]:
                    done("xn2tT", f"pe_qk_{i}_{s6}")

                def f_ev(e, i=i, s6=s6, pq=pq):
                    if s6 < 4:
                        dst = qkb[:, i % 2, s6 * 512:(s6 + 1) * 512]
                    else:
                        dst = vtmp[:, (s6 - 4) * 512:(s6 - 3) * 512]
                    return e.tensor_copy(dst, pq[:, 0:512])
                wts2 = [f"pe_qk_{i}_{s6}"]
                if s6 < 4:
                    wts2 += use(f"qkb{i % 2}", f"qkb{i % 2}_pe")
                else:
                    wts2 += use("vtmp")
                step("ve", f_ev, waits=wts2, tag=f"ve_qv_{i}_{s6}")
                done(pqn, f"ve_qv_{i}_{s6}")

            for dc in range(8) if "vtr" not in g.skip else []:
                slot = f"psT{dc % 4}"
                wts = [f"ve_qv_{i}_5"] + use(slot)
                if dc == 0:
                    wts += use("vtT")
                step("pe",
                     lambda e, i=i, dc=dc: e.matmul(
                         psT[dc % 4][:, 0:128],
                         vtmp[:, dc * 128:(dc + 1) * 128],
                         g.identb[:], start=True, stop=True),
                     waits=wts, tag=f"pe_vt_{i}_{dc}")
                step("ve",
                     lambda e, i=i, dc=dc: e.tensor_copy(
                         vtT[:, dc], psT[dc % 4][:, 0:128]),
                     waits=[f"pe_vt_{i}_{dc}"], tag=f"ve_vt_{i}_{dc}")
                done(slot, f"ve_vt_{i}_{dc}")
            if "vtr" not in g.skip:
                done("vtmp", f"pe_vt_{i}_7")
            if "vspill" not in g.skip and "vtr" not in g.skip:
                dma("sc",
                    g.dvspill[:].rearrange("p (c t) -> p c t", c=8)[
                        :, :, i * 128:(i + 1) * 128],
                    vtT[:], tag=f"sp_v_{i}", waits=[f"ve_vt_{i}_7"],
                    semname="s_vsp")
                done("vtT", f"sp_v_{i}")
            else:
                done("vtT", f"ve_vt_{i}_7")

            if i >= 1 and "gram" not in g.skip:
                def f_gram(e, i=i):
                    r = None
                    for h in range(16):
                        pg = psF if h < 8 else psG
                        r = e.matmul(
                            pg[0:64, h % 8],
                            qkb[:, i % 2, h * 64:(h + 1) * 64],
                            qkb[:, i % 2, 1024 + h * 64:1024 + (h + 1) * 64],
                            start=(i == 1 and h % 8 == 0),
                            stop=(i == NCH - 1 and h % 8 == 7),
                            skip_group_check=True)
                    return r
                step("pe", f_gram, waits=[f"ve_qv_{i}_3"],
                     tag=f"pe_gram_{i}")

                def f_sq(e, F, i=i):
                    r = None
                    for hh in range(2):
                        sl = slice(hh * 1024, (hh + 1) * 1024)
                        F(e.tensor_tensor(tmpw[:, 0:1024], qkb[:, i % 2, sl],
                                          qkb[:, i % 2, sl], op=ALU.mult))
                        if i == 1:
                            r = e.tensor_copy(qksq[:, sl], tmpw[:, 0:1024])
                        else:
                            r = e.tensor_tensor(qksq[:, sl], qksq[:, sl],
                                                tmpw[:, 0:1024], op=ALU.add)
                        if hh == 0:
                            F(r)
                    return r
                step("ve", f_sq, waits=[f"ve_qv_{i}_3"], tag=f"ve_sq_{i}")
                done(f"qkb{i % 2}", f"ve_sq_{i}")
                res[f"qkb{i % 2}_pe"] = f"pe_gram_{i}"

        if "qkv" in g.skip:
            cnt["cc_x0_gp"] = val("s_gp"); cnt["@cc_x0_gp"] = "s_gp"
            return_early = True
        for b32 in range(32) if ("qkv" not in g.skip and "nrm" not in g.skip) else []:
            slot = f"psT{b32 % 4}"
            wts = ([f"ve_sq_{NCH-1}"] if b32 == 0 else []) + use(slot)
            step("pe",
                 lambda e, b32=b32: e.matmul(
                     psT[b32 % 4][0:64, 0:1],
                     qksq[:, b32 * 64:(b32 + 1) * 64],
                     g.onescol[:], start=True, stop=True),
                 waits=wts, tag=f"pe_nrm_{b32}")
            step("ve",
                 lambda e, b32=b32: e.tensor_copy(
                     pack[0:64, 1024 + b32:1024 + b32 + 1],
                     psT[b32 % 4][0:64, 0:1]),
                 waits=[f"pe_nrm_{b32}"], tag=f"ve_nrm_{b32}")
            done(slot, f"ve_nrm_{b32}")

        def f_gram_evac(e):
            e.memset(pack[64:128, :], 0.0)
            for h in range(16):
                pg = psF if h < 8 else psG
                e.tensor_copy(pack[0:64, h * 64:(h + 1) * 64], pg[0:64, h % 8])
        if "qkv" not in g.skip and "nrm" not in g.skip:
            step("ve", f_gram_evac, waits=[f"pe_gram_{NCH-1}"],
                 tag="ve_gram_evac")

        for s in range(4) if ("qkv" not in g.skip and "nrm" not in g.skip) else []:
            wts = ["ve_gram_evac", "ve_nrm_31", "consts"] if s == 0 else \
                  [f"gp_cin0_{s-1}"]
            step("ve",
                 lambda e, s=s: e.tensor_scalar(
                     pack1[:], pack[:], g.indc[:, s:s + 1], None,
                     op0=ALU.mult),
                 waits=wts, tag=f"ve_pk_{s}")
            dma("gp", g.dcin0[:, s * SLOT0:(s + 1) * SLOT0], pack1[:],
                tag=f"gp_cin0_{s}", waits=[f"ve_pk_{s}"], semname="s_pk")

        def f_cc0(e):
            return e.collective_compute(
                "AllReduce", ALU.add, replica_groups=[list(range(8))],
                ins=[g.dcin0[:]], outs=[g.dcout0[:]])
        if "cc" not in g.skip and "qkv" not in g.skip and "nrm" not in g.skip:
            step("gp", f_cc0, waits=[f"gp_cin0_3"], dinc=None,
                 tag="cc_x0_gp")
        else:
            cnt["cc_x0_gp"] = val("s_gp"); cnt["@cc_x0_gp"] = "s_gp"
        @blk.gpsimd
        def _(e):
            pass

        if "dbg_c" in g.ddbg:
            dma("sc", g.ddbg["dbg_c"][:, 0:SLOT0], pack[:],
                waits=["ve_pk_3"])

    barrier()
    if "rest" in env["skip"]:
        return

    rest = ExitStack()
    env["rest_stack"] = rest
    xT = rest.enter_context(mk_sb("xT", [128, 8, TEXT], F32))
    g.xT = xT

    with ExitStack() as pa2:
        (xtok2,) = alloc(pa2, ("sb", "xtok2", [128, 2, 1024], F32))
        psT = [alloc(pa2, ("ps", f"psU{j}", [128, 512]))[0]
               for j in range(4)]
        for i in range(NCH):
            w = [f"ve_a2_{i-2}_7"] if i >= 2 else []
            dma("sy", xtok2[:, i % 2], g.dx[i * 128:(i + 1) * 128, :],
                tag=f"x2_ld_{i}", waits=w, semname=f"s_ld{i % 2}")
            for dc in range(8):
                k = i * 8 + dc
                wts = [f"x2_ld_{i}"] if dc == 0 else []
                prev = (i, dc - 4) if dc >= 4 else (i - 1, dc + 4)
                if prev[0] >= 0:
                    wts.append(f"ve_a2_{prev[0]}_{prev[1]}")
                step("pe",
                     lambda e, i=i, dc=dc: e.matmul(
                         psT[dc % 4][:, 0:128],
                         xtok2[:, i % 2, dc * 128:(dc + 1) * 128],
                         g.identf[:], start=True, stop=True),
                     waits=wts, tag=f"pe_a2_{i}_{dc}")
                step("ve",
                     lambda e, i=i, dc=dc: e.tensor_copy(
                         g.xT[:, dc, i * 128:(i + 1) * 128],
                         psT[dc % 4][:, 0:128]),
                     waits=[f"pe_a2_{i}_{dc}"], tag=f"ve_a2_{i}_{dc}")

    barrier()

    with ExitStack() as pb:
        (rw, rsc, rch, msk, sel, tmp64, rst, rgm, rmm) = alloc(
            pb, ("sb", "rw", [128, 17, 64], F32),
            ("sb", "rsc", [128, 64], F32), ("sb", "rch", [128, 64], F32),
            ("sb", "msk", [128, 64], F32), ("sb", "sel", [128, 64], F32),
            ("sb", "tmp64", [128, 64], F32), ("sb", "rst", [128, 64], F32),
            ("sb", "rgm", [128, 8], F32), ("sb", "rmm", [128, 1], F32))
        psL = [alloc(pb, ("ps", f"psL{j}", [128, 512]))[0] for j in range(2)]

        def g3(ap):
            return ap.rearrange("p (a b) -> p a b", a=8)

        def c3(ap):
            return ap.rearrange("p (a o) -> p a o", o=1)

        for i in range(NCH):
            def f_lg(e, i=i):
                r = None
                for dc in range(8):
                    r = e.matmul(psL[i % 2][:, 0:64],
                                 g.xnT[:, dc, i * 128:(i + 1) * 128],
                                 g.gwTs[:, dc], start=(dc == 0),
                                 stop=(dc == 7))
                return r
            step("pe", f_lg, waits=use(f"psL{i % 2}"), tag=f"pe_lg_{i}")

            def f_sig(e, i=i):
                return e.activation(rsc[:], psL[i % 2][:, 0:64], AF.Sigmoid)
            step("sc", f_sig, waits=[f"pe_lg_{i}"] + use("rsc"),
                 tag=f"sc_sig_{i}")
            done(f"psL{i % 2}", f"sc_sig_{i}")

            def f_route(e, F, i=i):
                F(e.tensor_tensor(rch[:], rsc[:], g.gbias[:], op=ALU.add))
                F(e.tensor_reduce(c3(rst[:, 0:8]), g3(rch[:]), axis=AX.X,
                                  op=ALU.max))
                m1b = c3(rst[:, 0:8]).broadcast_to((128, 8, 8))
                F(e.tensor_tensor(g3(msk[:]), g3(rch[:]), m1b,
                                  op=ALU.is_equal))
                F(e.tensor_reduce(c3(rst[:, 8:16]), g3(msk[:]), axis=AX.X,
                                  op=ALU.add))
                F(e.tensor_scalar(msk[:], msk[:], BIG, None, op0=ALU.mult))
                F(e.tensor_tensor(msk[:], rch[:], msk[:], op=ALU.subtract))
                F(e.tensor_reduce(c3(rst[:, 16:24]), g3(msk[:]), axis=AX.X,
                                  op=ALU.max))
                e.tensor_scalar(rst[:, 24:32], rst[:, 8:16], 2.0, None,
                                op0=ALU.is_ge)
                F(e.tensor_tensor(rst[:, 32:40], rst[:, 0:8], rst[:, 16:24],
                                  op=ALU.subtract))
                F(e.tensor_tensor(rst[:, 40:48], rst[:, 24:32],
                                  rst[:, 32:40], op=ALU.mult))
                F(e.tensor_tensor(rst[:, 48:56], rst[:, 0:8], rst[:, 16:24],
                                  op=ALU.add))
                F(e.tensor_tensor(rst[:, 48:56], rst[:, 48:56],
                                  rst[:, 40:48], op=ALU.add))
                F(e.memset(rgm[:], 0.0))
                for it in range(4):
                    F(e.tensor_reduce(rmm[:], rst[:, 48:56], axis=AX.X,
                                      op=ALU.max))
                    F(e.tensor_scalar(rst[:, 56:64], rst[:, 48:56], rmm[:],
                                      None, op0=ALU.is_equal))
                    e.tensor_tensor(rgm[:], rgm[:], rst[:, 56:64], op=ALU.max)
                    F(e.tensor_scalar(rst[:, 56:64], rst[:, 56:64], BIG,
                                      None, op0=ALU.mult))
                    F(e.tensor_tensor(rst[:, 48:56], rst[:, 48:56],
                                      rst[:, 56:64], op=ALU.subtract))
                gmb = c3(rgm[:]).broadcast_to((128, 8, 8))
                F(e.tensor_tensor(g3(msk[:]), g3(rch[:]), gmb, op=ALU.mult))
                F(e.memset(sel[:], 0.0))
                for it in range(8):
                    F(e.tensor_reduce(rmm[:], msk[:], axis=AX.X, op=ALU.max))
                    F(e.tensor_scalar(tmp64[:], msk[:], rmm[:], None,
                                      op0=ALU.is_equal))
                    e.tensor_tensor(sel[:], sel[:], tmp64[:], op=ALU.max)
                    F(e.tensor_scalar(tmp64[:], tmp64[:], BIG, None,
                                      op0=ALU.mult))
                    F(e.tensor_tensor(msk[:], msk[:], tmp64[:],
                                      op=ALU.subtract))
                F(e.tensor_tensor(tmp64[:], sel[:], rsc[:], op=ALU.mult))
                F(e.tensor_reduce(rmm[:], tmp64[:], axis=AX.X, op=ALU.add))
                F(e.tensor_scalar(rmm[:], rmm[:], g.epsc[:, 3:4], None,
                                  op0=ALU.add))
                F(e.reciprocal(rmm[:], rmm[:]))
                F(e.tensor_scalar(rmm[:], rmm[:], 2.5, None, op0=ALU.mult))
                return e.tensor_scalar(rw[:, i], tmp64[:], rmm[:], None,
                                       op0=ALU.mult)
            step("ve", f_route, waits=[f"sc_sig_{i}"], tag=f"ve_rt_{i}")
            done("rsc", f"ve_rt_{i}")

        for i in range(NCH):
            step("pe",
                 lambda e, i=i: e.matmul(psL[i % 2][0:64, 0:128],
                                         rw[:, i], g.identf[:],
                                         start=True, stop=True),
                 waits=[f"ve_rt_{i}"] + use(f"psL{i % 2}"),
                 tag=f"pe_rwt_{i}")
            step("ve",
                 lambda e, i=i: e.tensor_copy(
                     g.rwTb[0:64, i * 128:(i + 1) * 128],
                     psL[i % 2][0:64, 0:128]),
                 waits=[f"pe_rwt_{i}"], tag=f"ve_rwt_{i}")
            done(f"psL{i % 2}", f"ve_rwt_{i}")
        step("ve", lambda e: e.memset(g.rwTb[64:66, :], 1.0),
             tag="ve_rwshared")
        if "dbg_c" in g.ddbg:
            dma("sc", g.ddbg["dbg_c"][:, 0:17 * 64],
                rw[:].rearrange("p a b -> p (a b)"),
                waits=[f"ve_rt_{NCH-1}"])

    barrier()

    with ExitStack() as pd:
        gwb = [alloc(pd, ("sb", f"gwb{j}", [128, 8, EI], BF16))[0]
               for j in range(2)]
        uwb = [alloc(pd, ("sb", f"uwb{j}", [128, 8, EI], BF16))[0]
               for j in range(2)]
        dwb1, = alloc(pd, ("sb", "dwb", [128, 6, 1024], BF16))
        dwb = [dwb1, dwb1]
        rwb = [alloc(pd, ("sb", f"rwb{j}", [128, TEXT], BF16))[0]
               for j in range(2)]
        (mid, tA, tB, onesbr) = alloc(
            pd,
            ("sb", "mid", [128, 6, 1152], BF16),
            ("sb", "tA", [128, 512], BF16),
            ("sb", "tB", [128, 512], BF16),
            ("sb", "onesbr", [1, 128], BF16))
        rwrow = [alloc(pd, ("sb", f"rwrow{j}", [1, TEXT], BF16))[0]
                 for j in range(2)]
        psGU = [alloc(pd, ("ps", f"psGU{j}", [128, 1024]))[0]
                for j in range(2)]
        psDn = [alloc(pd, ("ps", f"psDn{j}", [128, 512]))[0]
                for j in range(2)]
        psR = [alloc(pd, ("ps", f"psR{j}", [128, 512]))[0] for j in range(2)]

        step("ve", lambda e: e.memset(onesbr[:], 1.0), tag="ve_ones1")

        def emit_wload(e_idx):
            par = e_idx % 2
            wts = use(f"wfree{par}")
            dma("gp", gwb[par][:].rearrange("p a b -> p (a b)"),
                g.dWg[e_idx], waits=wts, semname=f"s_w{'ab'[par]}")
            dma("gp", uwb[par][:].rearrange("p a b -> p (a b)"),
                g.dWu[e_idx], semname=f"s_w{'ab'[par]}",
                tag=f"wload_{e_idx}")

        def emit_dwload(e_idx):
            dma("gp", dwb1[:].rearrange("p a b -> p (a b)"),
                g.dWd[e_idx], waits=use("dwfree"), semname="s_vsp",
                tag=f"dwload_{e_idx}")

        emit_wload(0)
        emit_wload(1)
        emit_dwload(0)

        for ex in range(NE):
            par = ex % 2
            routed = ex < 64
            if routed:
                dma("sc", rwrow[par][:], g.rwTb[ex:ex + 1, :],
                    waits=use(f"rwrow{par}") +
                    (["ve_rwshared"] if ex < 2 else []),
                    semname=f"s_ex{par}", tag=f"rr_{ex}")
                for t5, (ts0, tw) in enumerate(TSL):
                    step("pe",
                         lambda e, ex=ex, ts0=ts0, tw=tw, par=par:
                         e.matmul(psR[par][:, 0:tw], onesbr[:],
                                  rwrow[par][0:1, ts0:ts0 + tw],
                                  start=True, stop=True),
                         waits=(use(f"psR{par}") + [f"rr_{ex}"] +
                                (["ve_ones1"] if ex < 2 else [])),
                         tag=f"pe_rb_{ex}_{t5}")
                    step("ve",
                         lambda e, ex=ex, ts0=ts0, tw=tw, par=par:
                         e.tensor_copy(rwb[par][:, ts0:ts0 + tw],
                                       psR[par][:, 0:tw]),
                         waits=[f"pe_rb_{ex}_{t5}"] + use(f"rwb{par}"),
                         tag=f"ve_rb_{ex}_{t5}")
                    done(f"psR{par}", f"ve_rb_{ex}_{t5}")
                done(f"rwrow{par}", f"pe_rb_{ex}_{4}")

            HALVES = [[(0, 512), (512, 512)],
                      [(1024, 512), (1536, 512), (2048, 128)]]
            kk = 0
            dk = 0
            for th, tlist in enumerate(HALVES):
                th0 = 1024 * th
                first_mid_of_half = kk
                for ic in range(6):
                    icw = ICS[ic]
                    ic0 = ic * 128
                    for (ts0, tw) in tlist:
                        gupar = kk % 2
                        wts = use(f"psGU{gupar}")
                        if kk == first_mid_of_half:
                            wts += [f"wload_{ex}"] if th == 0 else []
                            wts += use("mid_pe")
                        if routed and kk == 0:
                            wts += [f"ve_rb_{ex}_4"]

                        def f_gu(e, ex=ex, par=par, ic=ic, icw=icw, ic0=ic0,
                                 ts0=ts0, tw=tw, gupar=gupar):
                            for dc in range(8):
                                e.matmul(psGU[gupar][0:icw, 0:tw],
                                         gwb[par][:, dc, ic0:ic0 + icw],
                                         g.xnT[:, dc, ts0:ts0 + tw],
                                         start=(dc == 0), stop=(dc == 7))
                            r = None
                            for dc in range(8):
                                r = e.matmul(psGU[gupar][0:icw, 512:512 + tw],
                                             uwb[par][:, dc, ic0:ic0 + icw],
                                             g.xnT[:, dc, ts0:ts0 + tw],
                                             start=(dc == 0), stop=(dc == 7))
                            return r
                        step("pe", f_gu, waits=wts, tag=f"pe_gu_{ex}_{kk}")

                        def f_silu(e, icw=icw, tw=tw, gupar=gupar):
                            return e.activation(tA[0:icw, 0:tw],
                                                psGU[gupar][0:icw, 0:tw],
                                                AF.Silu)
                        step("sc", f_silu,
                             waits=[f"pe_gu_{ex}_{kk}"] + use("tA"),
                             tag=f"sc_sl_{ex}_{kk}")

                        def f_mid(e, F, routed=routed, ic=ic, icw=icw,
                                  ts0=ts0, tw=tw, gupar=gupar, par=par,
                                  th0=th0):
                            if routed:
                                F(e.tensor_tensor(
                                    tB[0:icw, 0:tw],
                                    psGU[gupar][0:icw, 512:512 + tw],
                                    rwb[par][0:icw, ts0:ts0 + tw],
                                    op=ALU.mult))
                            else:
                                F(e.tensor_copy(
                                    tB[0:icw, 0:tw],
                                    psGU[gupar][0:icw, 512:512 + tw]))
                            return e.tensor_tensor(
                                mid[0:icw, ic, ts0 - th0:ts0 - th0 + tw],
                                tA[0:icw, 0:tw], tB[0:icw, 0:tw],
                                op=ALU.mult)
                        step("ve", f_mid,
                             waits=[f"sc_sl_{ex}_{kk}"] + use("tB"),
                             tag=f"ve_mid_{ex}_{kk}")
                        done(f"psGU{gupar}", f"ve_mid_{ex}_{kk}")
                        done("tA", f"ve_mid_{ex}_{kk}")
                        done("tB", f"ve_mid_{ex}_{kk}")
                        kk += 1

                for oc in range(8):
                    for (ts0, tw) in tlist:
                        dpar = dk % 2
                        wts = use(f"psDn{dpar}")
                        if oc == 0 and (ts0, tw) == tlist[0]:
                            wts += [f"ve_mid_{ex}_{kk-1}", f"dwload_{ex}"]

                        def f_dn(e, par=par, oc=oc, ts0=ts0, tw=tw,
                                 dpar=dpar, th0=th0):
                            r = None
                            for ic in range(6):
                                icw = ICS[ic]
                                r = e.matmul(
                                    psDn[dpar][:, 0:tw],
                                    dwb1[0:icw, ic,
                                         oc * 128:(oc + 1) * 128],
                                    mid[0:icw, ic, ts0 - th0:ts0 - th0 + tw],
                                    start=(ic == 0), stop=(ic == 5))
                            return r
                        step("pe", f_dn, waits=wts, tag=f"pe_dn_{ex}_{dk}")

                        def f_acc(e, oc=oc, ts0=ts0, tw=tw, dpar=dpar):
                            return e.tensor_tensor(
                                g.xT[:, oc, ts0:ts0 + tw],
                                g.xT[:, oc, ts0:ts0 + tw],
                                psDn[dpar][:, 0:tw], op=ALU.add)
                        step("ve", f_acc, waits=[f"pe_dn_{ex}_{dk}"],
                             tag=f"ve_acc_{ex}_{dk}")
                        done(f"psDn{dpar}", f"ve_acc_{ex}_{dk}")
                        dk += 1
                done("mid_pe", f"pe_dn_{ex}_{dk-1}")
            done(f"rwb{par}", f"ve_mid_{ex}_{kk-1}")
            done(f"wfree{par}", f"pe_dn_{ex}_{dk-1}")
            done("dwfree", f"pe_dn_{ex}_{dk-1}")
            if ex + 2 < NE:
                emit_wload(ex + 2)
            if ex + 1 < NE:
                emit_dwload(ex + 1)

    barrier()

    with ExitStack() as pe_:
        (unp, unpP, woutb, vT, attP, attw, tmpodd, rkrow, etc2, selr,
         attb) = alloc(
            pe_,
            ("sb", "unp", [128, 4 * SLOT0], F32),
            ("sb", "unpP", [128, 544], F32),
            ("sb", "woutb", [128, 8, 1024], BF16),
            ("sb", "vT", [128, 8, TEXT], BF16),
            ("sb", "attP", [128, 8, 128], BF16),
            ("sb", "attw", [128, 8, 8], F32),
            ("sb", "tmpodd", [64, 8, 64], BF16),
            ("sb", "rkrow", [1, 128], F32),
            ("sb", "etc2", [128, 8], F32),
            ("sb", "selr", [1, 256], F32),
            ("sb", "attb", [128, 64], BF16))
        psE = [alloc(pe_, ("ps", f"psE{j}", [128, 512]))[0] for j in range(4)]

        dma("gp", woutb[:].rearrange("p a b -> p (a b)"), g.dwout[:],
            tag="ld_wout")
        dma("sy", vT[:].rearrange("p a b -> p (a b)"), g.dvspill[:],
            tag="ld_vT")
        dma("sy", selr[:], g.dselr[:], tag="ld_selr")
        dma("gp", unp[:], g.dcout0[:], waits=["cc_x0_gp"], tag="ld_unp")

        def f_slot(e, F):
            r = None
            for sidx in range(4):
                if sidx == 0:
                    r = e.tensor_scalar(unp[:, 0:SLOT0], unp[:, 0:SLOT0],
                                        g.indc[:, 0:1], None, op0=ALU.mult)
                else:
                    r = e.scalar_tensor_tensor(
                        unp[:, 0:SLOT0],
                        unp[:, sidx * SLOT0:(sidx + 1) * SLOT0],
                        g.indc[:, sidx:sidx + 1], unp[:, 0:SLOT0],
                        op0=ALU.mult, op1=ALU.add)
                if sidx < 3:
                    F(r)
            return r
        step("ve", f_slot, waits=["ld_unp", "consts"], tag="ve_slot")

        def f_evn(e):
            for pp in range(8):
                e.tensor_copy(unpP[0:64, pp * 64:(pp + 1) * 64],
                              unp[0:64, (2 * pp) * 64:(2 * pp + 1) * 64])
                e.tensor_copy(unpP[0:64, 512 + pp:513 + pp],
                              unp[0:64, 1024 + 2 * pp:1025 + 2 * pp])
                e.tensor_copy(unpP[0:64, 520 + pp:521 + pp],
                              unp[0:64, 1040 + 2 * pp:1041 + 2 * pp])
            return True
        step("ve", f_evn, waits=["ve_slot"], tag="ve_evn")
        dma("sc",
            unpP[64:128, 0:512].rearrange("p (a b) -> p a b", b=64),
            unp[0:64, 0:1024].rearrange("p (a b) -> p a b", b=64)[:, 1::2, :],
            waits=["ve_slot"], tag="sp_odd1", semname="s_sk0")
        dma("sc",
            unpP[64:128, 512:520].rearrange("p (a o) -> p a o", o=1),
            unp[0:64, 1024:1040].rearrange("p (a b) -> p a b", b=2)[:, :, 1:2],
            waits=["ve_slot"], tag="sp_odd2", semname="s_sk1", slow=True)
        dma("sc",
            unpP[64:128, 520:528].rearrange("p (a o) -> p a o", o=1),
            unp[0:64, 1040:1056].rearrange("p (a b) -> p a b", b=2)[:, :, 1:2],
            waits=["ve_slot"], tag="sp_odd3", semname="s_vsp", slow=True)

        step("sc", lambda e: e.activation(etc2[:], g.tempc[:], AF.Exp),
             waits=["consts"], tag="sc_etc")
        step("sc",
             lambda e: e.activation(attw[:].rearrange("p a b -> p (a b)")[
                 :, 48:64], unpP[:, 512:528], AF.Sqrt),
             waits=["sp_odd2", "sp_odd3", "ve_evn"], tag="sc_nsq")
        step("ve", lambda e: e.memset(attP[:], 0.0), tag="ve_attP0")

        for pp in range(8):
            wk = ["sp_odd1", "sc_etc", "ld_selr"] if pp == 0 else []

            aflat = attw[:].rearrange("p a b -> p (a b)")

            def f_rqk(e, F, pp=pp, aflat=aflat):
                t0 = attw[:, pp, 0:1]
                t1 = attw[:, pp, 1:2]
                F(e.tensor_scalar(t0, aflat[:, 48 + pp:49 + pp],
                                  g.epsc[:, 5:6], None, op0=ALU.max))
                F(e.reciprocal(t0, t0))
                F(e.tensor_scalar(t1, aflat[:, 56 + pp:57 + pp],
                                  g.epsc[:, 5:6], None, op0=ALU.max))
                return e.reciprocal(t1, t1)
            step("ve", f_rqk, waits=wk + ["sc_nsq"], tag=f"ve_rqk_{pp}")

            step("pe",
                 lambda e, pp=pp: e.matmul(
                     psE[0][0:1, 0:128], attw[:, pp, 1:2], g.identf[:],
                     start=True, stop=True),
                 waits=[f"ve_rqk_{pp}"] + use("psE0"), tag=f"pe_rkT_{pp}")
            step("ve",
                 lambda e, pp=pp: e.tensor_copy(rkrow[:], psE[0][0:1, 0:128]),
                 waits=[f"pe_rkT_{pp}"] + use("rkrow"), tag=f"ve_rkT_{pp}")
            done("psE0", f"ve_rkT_{pp}")

            def f_rkb(e, pp=pp):
                e.matmul(psE[1][:, 0:64], selr[0:1, 0:128], rkrow[0:1, 0:64],
                         start=True, stop=False)
                return e.matmul(psE[1][:, 0:64], selr[0:1, 128:256],
                                rkrow[0:1, 64:128], start=False, stop=True)
            step("pe", f_rkb, waits=[f"ve_rkT_{pp}"] + use("psE1"),
                 tag=f"pe_rkb_{pp}")
            done("rkrow", f"pe_rkb_{pp}")

            def f_sm(e, F, pp=pp):
                t2 = attw[:, pp, 2:3]
                t3 = attw[:, pp, 3:4]
                sim = unpP[:, pp * 64:(pp + 1) * 64]
                F(e.tensor_scalar(sim, sim, attw[:, pp, 0:1], None,
                                  op0=ALU.mult))
                F(e.tensor_tensor(sim, sim, psE[1][:, 0:64], op=ALU.mult))
                F(e.tensor_scalar(sim, sim, etc2[:, pp:pp + 1], 8.0,
                                  op0=ALU.mult, op1=ALU.mult))
                F(e.tensor_reduce(t2, sim, axis=AX.X, op=ALU.max))
                return e.tensor_scalar(sim, sim, t2, None, op0=ALU.subtract)
            step("ve", f_sm, waits=[f"pe_rkb_{pp}"], tag=f"ve_sm_{pp}")

            step("sc",
                 lambda e, pp=pp: e.activation(
                     unpP[:, pp * 64:(pp + 1) * 64],
                     unpP[:, pp * 64:(pp + 1) * 64], AF.Exp),
                 waits=[f"ve_sm_{pp}"], tag=f"sc_exp_{pp}")
            done("psE1", f"sc_exp_{pp}")

            def f_att(e, F, pp=pp):
                t3 = attw[:, pp, 3:4]
                sim = unpP[:, pp * 64:(pp + 1) * 64]
                F(e.tensor_reduce(t3, sim, axis=AX.X, op=ALU.add))
                F(e.reciprocal(t3, t3))
                return e.tensor_scalar(attb[:], sim, t3, None, op0=ALU.mult)
            step("ve", f_att, waits=[f"sc_exp_{pp}"] + use("attb"),
                 tag=f"ve_att_{pp}")

            step("pe",
                 lambda e, pp=pp: e.matmul(psE[2][0:64, 0:128], attb[:],
                                           g.identb[:], start=True,
                                           stop=True),
                 waits=[f"ve_att_{pp}"] + use("psE2"), tag=f"pe_atT_{pp}")
            done("attb", f"pe_atT_{pp}")

            def f_atev(e, pp=pp):
                e.tensor_copy(attP[0:64, pp, 0:64], psE[2][0:64, 0:64])
                return e.tensor_copy(tmpodd[0:64, pp, :],
                                     psE[2][0:64, 64:128])
            step("ve", f_atev, waits=[f"pe_atT_{pp}", "ve_attP0"],
                 tag=f"ve_atev_{pp}")
            done("psE2", f"ve_atev_{pp}")

        dma("sc", attP[64:128, :, 64:128], tmpodd[:],
            waits=[f"ve_atev_7"], tag="sp_attodd", semname="s_sk0")

        for pp in range(8):
            for t5, (ts0, tw) in enumerate(TSL):
                wts = ["sp_attodd", "ld_vT"] if (pp == 0 and t5 == 0) else []
                wts += use(f"psE{t5 % 2}")
                step("pe",
                     lambda e, pp=pp, ts0=ts0, tw=tw, t5=t5: e.matmul(
                         psE[t5 % 2][:, 0:tw], attP[:, pp, :],
                         vT[:, pp, ts0:ts0 + tw], start=True, stop=True),
                     waits=wts, tag=f"pe_ho_{pp}_{t5}")
                step("ve",
                     lambda e, pp=pp, ts0=ts0, tw=tw, t5=t5: e.tensor_copy(
                         vT[:, pp, ts0:ts0 + tw], psE[t5 % 2][:, 0:tw]),
                     waits=[f"pe_ho_{pp}_{t5}"], tag=f"ve_ho_{pp}_{t5}")
                done(f"psE{t5 % 2}", f"ve_ho_{pp}_{t5}")

        for oc in range(8):
            for t5, (ts0, tw) in enumerate(TSL):
                wts = use(f"psE{2 + (oc * 5 + t5) % 2}")
                if oc == 0:
                    wts += [f"ve_ho_{pp2}_{t5}" for pp2 in range(8)]                         if t5 == 0 else [f"ve_ho_7_{t5}"]
                if oc == 0 and t5 == 0:
                    wts += ["ld_wout"]

                def f_wo(e, oc=oc, ts0=ts0, tw=tw, t5=t5):
                    ps = psE[2 + (oc * 5 + t5) % 2]
                    r = None
                    for cc in range(8):
                        r = e.matmul(ps[:, 0:tw],
                                     woutb[:, cc, oc * 128:(oc + 1) * 128],
                                     vT[:, cc, ts0:ts0 + tw],
                                     start=(cc == 0), stop=(cc == 7))
                    return r
                step("pe", f_wo, waits=wts, tag=f"pe_wo_{oc}_{t5}")
                step("ve",
                     lambda e, oc=oc, ts0=ts0, tw=tw, t5=t5: e.tensor_tensor(
                         g.xT[:, oc, ts0:ts0 + tw],
                         g.xT[:, oc, ts0:ts0 + tw],
                         psE[2 + (oc * 5 + t5) % 2][:, 0:tw], op=ALU.add),
                     waits=[f"pe_wo_{oc}_{t5}"], tag=f"ve_wo_{oc}_{t5}")
                done(f"psE{2 + (oc * 5 + t5) % 2}", f"ve_wo_{oc}_{t5}")

    barrier()

    with ExitStack() as pf:
        (lnst, xn1ck, junk2) = alloc(
            pf,
            ("sb", "lnst", [128, 2, 24], F32),
            ("sb", "xn1ck", [128, 2, 1024], BF16),
            ("sb", "junk2", [128, 2, 1024], BF16))
        psJ = [alloc(pf, ("ps", f"psJ{j}", [128, 512]))[0] for j in range(4)]
        psK = [alloc(pf, ("ps", f"psK{j}", [128, 512]))[0] for j in range(2)]

        for i in range(NCH):
            jpar = i % 2
            st2 = lnst[:, jpar]
            for dc in range(8):
                wts = use(f"psJ{2 * jpar + dc // 4}") if dc % 4 == 0 else []
                step("pe",
                     lambda e, i=i, dc=dc, jpar=jpar: e.matmul(
                         psJ[2 * jpar + dc // 4][:, (dc % 4) * 128:
                                                 (dc % 4 + 1) * 128],
                         g.xT[:, dc, i * 128:(i + 1) * 128], g.identf[:],
                         start=True, stop=True),
                     waits=wts, tag=f"pe_f_{i}_{dc}")

            def f_fsum(e, F, i=i, jpar=jpar, st2=st2):
                r = None
                for dc in range(8):
                    ps = psJ[2 * jpar + dc // 4][:, (dc % 4) * 128:
                                                 (dc % 4 + 1) * 128]
                    e.activation(junk2[:, 0, 0:128], ps, AF.Copy,
                                 accum_out=st2[:, dc:dc + 1])
                    r = e.activation(junk2[:, 1, 0:128], ps, AF.Square,
                                     accum_out=st2[:, 8 + dc:9 + dc])
                return r
            step("sc", f_fsum,
                 waits=[f"pe_f_{i}_7"] + use(f"lnst{jpar}"),
                 tag=f"sc_f_{i}")

            def f_fstat(e, F, st2=st2):
                F(e.tensor_reduce(st2[:, 16:17], st2[:, 0:8], axis=AX.X,
                                  op=ALU.add))
                F(e.tensor_reduce(st2[:, 17:18], st2[:, 8:16], axis=AX.X,
                                  op=ALU.add))
                F(e.tensor_scalar(st2[:, 18:19], st2[:, 16:17], 1.0 / 1024.0,
                                  None, op0=ALU.mult))
                F(e.tensor_scalar(st2[:, 19:20], st2[:, 17:18], 1.0 / 1024.0,
                                  None, op0=ALU.mult))
                F(e.tensor_tensor(st2[:, 20:21], st2[:, 18:19],
                                  st2[:, 18:19], op=ALU.mult))
                return e.tensor_tensor(st2[:, 21:22], st2[:, 19:20],
                                       st2[:, 20:21], op=ALU.subtract)
            step("ve", f_fstat, waits=[f"sc_f_{i}"], tag=f"ve_fv_{i}")

            step("sc",
                 lambda e, st2=st2: e.activation(st2[:, 22:23],
                                                 st2[:, 21:22], AF.Sqrt,
                                                 bias=g.epsc[:, 0:1]),
                 waits=[f"ve_fv_{i}"], tag=f"sc_fs_{i}")

            def f_frs(e, st2=st2):
                return e.reciprocal(st2[:, 23:24], st2[:, 22:23])
            step("ve", f_frs, waits=[f"sc_fs_{i}"], tag=f"ve_frs_{i}")

            def f_fnorm(e, F, i=i, jpar=jpar, st2=st2):
                r = None
                for dc in range(8):
                    ps = psJ[2 * jpar + dc // 4][:, (dc % 4) * 128:
                                                 (dc % 4 + 1) * 128]
                    r = e.tensor_scalar(
                        xn1ck[:, jpar, dc * 128:(dc + 1) * 128], ps,
                        st2[:, 18:19], st2[:, 23:24],
                        op0=ALU.subtract, op1=ALU.mult)
                return r
            step("ve", f_fnorm, waits=[f"ve_frs_{i}"] + use(f"xn1ck{jpar}"),
                 tag=f"ve_fn_{i}")
            done(f"lnst{jpar}", f"ve_fn_{i}")
            done(f"psJ{2 * jpar}", f"ve_fn_{i}")
            done(f"psJ{2 * jpar + 1}", f"ve_fn_{i}")

            for dc in range(8):
                kslot = dc % 2
                wts = [f"ve_fn_{i}"] if dc == 0 else []
                wts += use(f"psK{kslot}")
                step("pe",
                     lambda e, i=i, dc=dc, jpar=jpar, kslot=kslot: e.matmul(
                         psK[kslot][:, 0:128],
                         xn1ck[:, jpar, dc * 128:(dc + 1) * 128],
                         g.identb[:], start=True, stop=True),
                     waits=wts, tag=f"pe_fn_{i}_{dc}")
                step("ve",
                     lambda e, i=i, dc=dc, kslot=kslot: e.tensor_copy(
                         g.xnT[:, dc, i * 128:(i + 1) * 128],
                         psK[kslot][:, 0:128]),
                     waits=[f"pe_fn_{i}_{dc}"], tag=f"ve_fnt_{i}_{dc}")
                done(f"psK{kslot}", f"ve_fnt_{i}_{dc}")
            done(f"xn1ck{jpar}", f"pe_fn_{i}_7")

    barrier()

    with ExitStack() as ph1:
        (encqb, qTf, sigs, sig4) = alloc(
            ph1,
            ("sb", "encqb", [128, 8, 1024], BF16),
            ("sb", "qTf", [128, 2, TEXT], F32),
            ("sb", "sigs", [128, 16], F32),
            ("sb", "sig4", [128, 64], F32))
        psQ = [alloc(ph1, ("ps", f"psQ{j}", [128, 512]))[0] for j in range(2)]

        dma("gp", encqb[:].rearrange("p a b -> p (a b)"), g.dencq[:],
            tag="ld_encq")
        for h in range(NH):
            hp = h % 2
            for t5, (ts0, tw) in enumerate(TSL):
                wts = use(f"psQ{t5 % 2}")
                if h == 0 and t5 == 0:
                    wts += ["ld_encq"]

                def f_q(e, h=h, ts0=ts0, tw=tw, t5=t5):
                    r = None
                    for dc in range(8):
                        r = e.matmul(psQ[t5 % 2][:, 0:tw],
                                     encqb[:, dc, h * 128:(h + 1) * 128],
                                     g.xnT[:, dc, ts0:ts0 + tw],
                                     start=(dc == 0), stop=(dc == 7))
                    return r
                step("pe", f_q, waits=wts, tag=f"pe_q_{h}_{t5}")
                step("sc",
                     lambda e, h=h, hp=hp, ts0=ts0, tw=tw, t5=t5:
                     e.activation(qTf[:, hp, ts0:ts0 + tw],
                                  psQ[t5 % 2][:, 0:tw], AF.Relu),
                     waits=[f"pe_q_{h}_{t5}"] +
                     (use(f"qTf{hp}") if t5 == 0 else []),
                     tag=f"sc_q_{h}_{t5}")
                done(f"psQ{t5 % 2}", f"sc_q_{h}_{t5}")

            def f_sig(e, F, h=h, hp=hp):
                F(e.tensor_reduce(sigs[:, h:h + 1], qTf[:, hp, 128:TEXT],
                                  axis=AX.X, op=ALU.add))
                stv = qTf[:, hp, 128:TEXT].rearrange(
                    "p (a b) -> p a b", b=8)[:, :, 0:1]
                return e.tensor_reduce(
                    sigs[:, 8 + h:9 + h],
                    stv.rearrange("p a o -> p (a o)"), axis=AX.X, op=ALU.add)
            step("ve", f_sig, waits=[f"sc_q_{h}_4"], tag=f"ve_sig_{h}")

            dma("sc", g.dqspill[h], qTf[:, hp],
                waits=[f"ve_sig_{h}"] + (use(f"qsp{hp}") if False else []),
                tag=f"sp_q_{h}", semname=f"s_sk{hp}")
            done(f"qTf{hp}", f"sp_q_{h}")

        def f_sig4(e, F):
            r = None
            for sidx in range(4):
                r = e.tensor_scalar(sig4[:, sidx * 16:(sidx + 1) * 16],
                                    sigs[:], g.indc[:, sidx:sidx + 1], None,
                                    op0=ALU.mult)
                if sidx < 3:
                    F(r)
            return r
        step("ve", f_sig4, waits=[f"ve_sig_{NH-1}"], tag="ve_sig4")
        dma("gp", g.dcin1[:], sig4[:], waits=["ve_sig4"], tag="gp_cin1",
            semname="s_pk")

        def f_cc1(e):
            return e.collective_compute(
                "AllReduce", ALU.add, replica_groups=[list(range(8))],
                ins=[g.dcin1[:]], outs=[g.dcout1[:]])
        step("gp", f_cc1, waits=["gp_cin1"], tag="cc_x1_gp")

    barrier()

    with ExitStack() as pg:
        (vlrb, fbTb, gatec, P1b, P2g, sc1r, sc2r, sc1c, sc2c, uTs) = \
            alloc(
            pg,
            ("sb", "vlrb", [128, 128], BF16),
            ("sb", "fbTb", [16, 16], BF16),
            ("sb", "gatec", [16, 1], F32),
            ("sb", "P1b", [16, TEXT], BF16),
            ("sb", "P2g", [16, TEXT], BF16),
            ("sb", "sc1r", [1, 4096], F32),
            ("sb", "sc2r", [1, 1024], F32),
            ("sb", "sc1c", [128, 32], F32),
            ("sb", "sc2c", [128, 8], F32),
            ("sb", "uTs", [16, 4096], BF16))
        psGa = [alloc(pg, ("ps", f"psGa{j}", [128, 512]))[0] for j in range(2)]
        psGb = [alloc(pg, ("ps", f"psGb{j}", [128, 512]))[0] for j in range(2)]
        psGc, = alloc(pg, ("ps", "psGc", [128, 512]))

        dma("gp", vlrb[:], g.dvlr[:], tag="ld_vlr")
        dma("gp", fbTb[:], g.dfbT[:], tag="ld_fbT")
        dma("sy", gatec[:], g.dgate[:], tag="ld_gate")

        for t5, (ts0, tw) in enumerate(TSL):
            def f_p1(e, ts0=ts0, tw=tw):
                r = None
                for dc in range(8):
                    r = e.matmul(psGc[0:16, 0:tw],
                                 vlrb[:, dc * 16:(dc + 1) * 16],
                                 g.xnT[:, dc, ts0:ts0 + tw],
                                 start=(dc == 0), stop=(dc == 7))
                return r
            step("pe", f_p1,
                 waits=(["ld_vlr"] if t5 == 0 else []) + use("psGc"),
                 tag=f"pe_p1_{t5}")
            step("ve",
                 lambda e, ts0=ts0, tw=tw: e.tensor_copy(
                     P1b[0:16, ts0:ts0 + tw], psGc[0:16, 0:tw]),
                 waits=[f"pe_p1_{t5}"], tag=f"ve_p1_{t5}")
            done("psGc", f"ve_p1_{t5}")
        for t5, (ts0, tw) in enumerate(TSL):
            def f_p2(e, ts0=ts0, tw=tw):
                return e.matmul(psGc[0:16, 0:tw], fbTb[:],
                                P1b[0:16, ts0:ts0 + tw], start=True,
                                stop=True)
            step("pe", f_p2,
                 waits=(["ve_p1_4", "ld_fbT"] if t5 == 0 else []) +
                 use("psGc"), tag=f"pe_p2_{t5}")
            step("ve",
                 lambda e, ts0=ts0, tw=tw: e.tensor_scalar(
                     P2g[0:16, ts0:ts0 + tw], psGc[0:16, 0:tw],
                     gatec[:], None, op0=ALU.mult),
                 waits=[f"pe_p2_{t5}"] + (["ld_gate"] if t5 == 0 else []),
                 tag=f"ve_p2_{t5}")
            done("psGc", f"ve_p2_{t5}")

        with ExitStack() as psw:
            (wstage, absw, wst2, abs2, uTf) = alloc(
                psw,
                ("sb", "wstage", [128, 4, 512], F32),
                ("sb", "absw", [128, 4, 512], F32),
                ("sb", "wst2", [128, 2, 1024], F32),
                ("sb", "abs2", [128, 2, 1024], F32),
                ("sb", "uTf", [16, 4096], F32))
            psS, = alloc(psw, ("ps", "psS", [128, 1024]))
            dma("gp", uTf[:], g.duT[:], tag="ld_uT")
            for hc2 in range(16):
                hc, wh = hc2 // 2, hc2 % 2
                dma("gp", wstage[:],
                    g.dwfcT[wh * 512:(wh + 1) * 512,
                            hc * 512:(hc + 1) * 512].rearrange(
                        "(a p) b -> p a b", p=128),
                    waits=use("wstage"), tag=f"ld_s1_{hc2}", semname="s_wa")
                step("sc",
                     lambda e: e.activation(
                         absw[:].rearrange("p a b -> p (a b)"),
                         wstage[:].rearrange("p a b -> p (a b)"), AF.Abs),
                     waits=[f"ld_s1_{hc2}"] + use("absw"),
                     tag=f"sc_s1_{hc2}")
                done("wstage", f"sc_s1_{hc2}")

                def f_s1(e, hc2=hc2, wh=wh):
                    r = None
                    for dc in range(4):
                        r = e.matmul(psGc[:, 0:512], g.onessq[:],
                                     absw[:, dc], start=(wh == 0 and dc == 0),
                                     stop=(wh == 1 and dc == 3))
                    return r
                step("pe", f_s1, waits=[f"sc_s1_{hc2}"] +
                     (use("psGc") if wh == 0 else []), tag=f"pe_s1_{hc2}")
                done("absw", f"pe_s1_{hc2}")

                if wh == 1:
                    def f_s1e(e, F, hc=hc):
                        F(e.tensor_scalar(sc1r[0:1, hc * 512:(hc + 1) * 512],
                                          psGc[0:1, 0:512], 1.0 / 1024.0,
                                          None, op0=ALU.mult))
                        return e.tensor_scalar(
                            sc1r[0:1, hc * 512:(hc + 1) * 512],
                            sc1r[0:1, hc * 512:(hc + 1) * 512], 1e-5, None,
                            op0=ALU.max)
                    step("ve", f_s1e, waits=[f"pe_s1_{hc2}"],
                         tag=f"ve_s1_{hc}")
                    done("psGc", f"ve_s1_{hc}")
            for hc2 in range(16):
                dma("gp", wst2[:],
                    g.dwprT[hc2 * 256:(hc2 + 1) * 256, :].rearrange(
                        "(a p) b -> p a b", p=128),
                    waits=use("wst2"), tag=f"ld_s2_{hc2}", semname="s_wb")
                step("sc",
                     lambda e: e.activation(
                         abs2[:].rearrange("p a b -> p (a b)"),
                         wst2[:].rearrange("p a b -> p (a b)"), AF.Abs),
                     waits=[f"ld_s2_{hc2}"] + use("abs2"),
                     tag=f"sc_s2_{hc2}")
                done("wst2", f"sc_s2_{hc2}")

                def f_s2(e, hc2=hc2):
                    r = None
                    for dc in range(2):
                        for hf in range(2):
                            r = e.matmul(
                                psS[:, hf * 512:(hf + 1) * 512],
                                g.onessq[:],
                                abs2[:, dc, hf * 512:(hf + 1) * 512],
                                start=(hc2 == 0 and dc == 0),
                                stop=(hc2 == 15 and dc == 1),
                                skip_group_check=True)
                    return r
                step("pe", f_s2, waits=[f"sc_s2_{hc2}"], tag=f"pe_s2_{hc2}")
                done("abs2", f"pe_s2_{hc2}")

            def f_s2e(e, F):
                F(e.tensor_scalar(sc2r[0:1, 0:1024], psS[0:1, 0:1024],
                                  1.0 / 4096.0, None, op0=ALU.mult))
                F(e.tensor_scalar(sc2r[0:1, 0:1024], sc2r[0:1, 0:1024],
                                  1e-5, None, op0=ALU.max))
                return e.nop()
            step("ve", f_s2e, waits=[f"pe_s2_15"], tag="ve_s2e")

            for j in range(32):
                wts = use("psGc") + (["ve_s1_7"] if j == 0 else [])
                step("pe",
                     lambda e, j=j: e.matmul(
                         psGc[:, 0:128], sc1r[0:1, j * 128:(j + 1) * 128],
                         g.identf[0:1, 0:128], start=True, stop=True),
                     waits=wts, tag=f"pe_c1_{j}")
                step("ve",
                     lambda e, j=j: e.tensor_copy(sc1c[:, j:j + 1],
                                                  psGc[:, 0:1]),
                     waits=[f"pe_c1_{j}"], tag=f"ve_c1_{j}")
                done("psGc", f"ve_c1_{j}")
            for j in range(8):
                wts = use("psGc") + (["ve_s2e"] if j == 0 else [])
                step("pe",
                     lambda e, j=j: e.matmul(
                         psGc[:, 0:128], sc2r[0:1, j * 128:(j + 1) * 128],
                         g.identf[0:1, 0:128], start=True, stop=True),
                     waits=wts, tag=f"pe_c2_{j}")
                step("ve",
                     lambda e, j=j: e.tensor_copy(sc2c[:, j:j + 1],
                                                  psGc[:, 0:1]),
                     waits=[f"pe_c2_{j}"], tag=f"ve_c2_{j}")
                done("psGc", f"ve_c2_{j}")

            def f_rcp(e, F):
                F(e.reciprocal(sc1r[0:1, 0:4096], sc1r[0:1, 0:4096]))
                return e.reciprocal(sc2r[0:1, 0:1024], sc2r[0:1, 0:1024])
            step("ve", f_rcp, waits=["ve_c1_31", "ve_c2_7"], tag="ve_rcp")

            for j in range(8):
                wts = use("psGc") + (["ve_rcp", "ld_uT"] if j == 0 else [])
                step("pe",
                     lambda e, j=j: e.matmul(
                         psGc[0:16, 0:512], g.onessq[0:1, 0:16],
                         sc1r[0:1, j * 512:(j + 1) * 512],
                         start=True, stop=True),
                     waits=wts, tag=f"pe_u_{j}")
                step("ve",
                     lambda e, j=j: e.tensor_tensor(
                         uTs[0:16, j * 512:(j + 1) * 512],
                         uTf[0:16, j * 512:(j + 1) * 512],
                         psGc[0:16, 0:512], op=ALU.mult),
                     waits=[f"pe_u_{j}"], tag=f"ve_u_{j}")
                done("psGc", f"ve_u_{j}")

        with ExitStack() as pmn:
            (wstage, wst2, ternb, tern2b, r2slab, zt, tscr, rbuf) = alloc(
                pmn,
                ("sb", "wstageM", [128, 4, 512], F32),
                ("sb", "wst2M", [128, 2, 1024], F32),
                ("sb", "ternb", [128, 8, 512], BF16),
                ("sb", "tern2b", [128, 4, 1024], BF16),
                ("sb", "r2slab", [128, 4, TEXT], BF16),
                ("sb", "zt", [128, 512], F32),
                ("sb", "tscr", [128, 2, 512], F32),
                ("sb", "rbuf", [128, 2, 512], F32))
            psT2, = alloc(pmn, ("ps", "psT2G", [128, 1024]))

            for hc in range(8):
                for wh in range(2):
                    dma("gp", wstage[:],
                        g.dwfcT[wh * 512:(wh + 1) * 512,
                                hc * 512:(hc + 1) * 512].rearrange(
                            "(a p) b -> p a b", p=128),
                        waits=use("wstage"), tag=f"ld_m1_{hc}_{wh}",
                        semname="s_wa")
                    if wh == 0:
                        step("pe",
                             lambda e, hc=hc: e.matmul(
                                 psGc[:, 0:512], g.onessq[0:1, :],
                                 sc1r[0:1, hc * 512:(hc + 1) * 512],
                                 start=True, stop=True),
                             waits=use("psGc"), tag=f"pe_b1_{hc}")

                    def f_t1(e, F, hc=hc, wh=wh):
                        r = None
                        for j in range(4):
                            dc = wh * 4 + j
                            F(e.tensor_tensor(zt[:], wstage[:, j],
                                              psGc[:, 0:512], op=ALU.mult))
                            F(e.tensor_scalar(tscr[:, 0], zt[:], 0.5, None,
                                              op0=ALU.is_gt))
                            F(e.tensor_scalar(tscr[:, 1], zt[:], -0.5, None,
                                              op0=ALU.is_lt))
                            r = e.tensor_tensor(ternb[:, dc], tscr[:, 0],
                                                tscr[:, 1], op=ALU.subtract)
                            if j < 3:
                                F(r)
                        return r
                    step("ve", f_t1,
                         waits=[f"pe_b1_{hc}", f"ld_m1_{hc}_{wh}"] +
                         use("ternb" if wh == 0 else "ternb_nop"),
                         tag=f"ve_t1_{hc}_{wh}")
                    done("wstage", f"ve_t1_{hc}_{wh}")
                done("psGc", f"ve_t1_{hc}_1")

                def f_b2(e, hc=hc):
                    e.matmul(psT2[:, 0:512], g.onessq[0:1, :],
                             sc2r[0:1, 0:512], start=True, stop=True)
                    return e.matmul(psT2[:, 512:1024], g.onessq[0:1, :],
                                    sc2r[0:1, 512:1024], start=True,
                                    stop=True)
                step("pe", f_b2, waits=use("psT2"), tag=f"pe_b2_{hc}")

                for wh in range(2):
                    dma("gp", wst2[:],
                        g.dwprT[hc * 512 + wh * 256:
                                hc * 512 + (wh + 1) * 256, :].rearrange(
                            "(a p) b -> p a b", p=128),
                        waits=use("wst2"), tag=f"ld_m2_{hc}_{wh}",
                        semname="s_wb")

                    def f_t2(e, F, hc=hc, wh=wh):
                        r = None
                        for j in range(2):
                            hsub = wh * 2 + j
                            for hf in range(2):
                                F(e.tensor_tensor(
                                    zt[:], wst2[:, j, hf * 512:(hf + 1) * 512],
                                    psT2[:, hf * 512:(hf + 1) * 512],
                                    op=ALU.mult))
                                F(e.tensor_scalar(tscr[:, 0], zt[:], 0.5,
                                                  None, op0=ALU.is_gt))
                                F(e.tensor_scalar(tscr[:, 1], zt[:], -0.5,
                                                  None, op0=ALU.is_lt))
                                r = e.tensor_tensor(
                                    tern2b[:, hsub, hf * 512:(hf + 1) * 512],
                                    tscr[:, 0], tscr[:, 1], op=ALU.subtract)
                                if not (j == 1 and hf == 1):
                                    F(r)
                        return r
                    step("ve", f_t2,
                         waits=[f"pe_b2_{hc}", f"ld_m2_{hc}_{wh}"] +
                         (use("tern2b") if wh == 0 else []),
                         tag=f"ve_t2_{hc}_{wh}")
                    done("wst2", f"ve_t2_{hc}_{wh}")
                done("psT2", f"ve_t2_{hc}_1")

                kk = 0
                for hsub in range(4):
                    for t5, (ts0, tw) in enumerate(TSL):
                        apar = kk % 2
                        wts = use(f"psGa{apar}")
                        if kk == 0:
                            wts += [f"ve_t1_{hc}_1", "ve_u_7"]
                            wts += use("r2_pe")

                        def f_hp(e, hc=hc, hsub=hsub, ts0=ts0, tw=tw,
                                 apar=apar):
                            for dc in range(8):
                                e.matmul(
                                    psGa[apar][:, 0:tw],
                                    ternb[:, dc, hsub * 128:(hsub + 1) * 128],
                                    g.xnT[:, dc, ts0:ts0 + tw],
                                    start=(dc == 0), stop=False)
                            return e.matmul(
                                psGa[apar][:, 0:tw],
                                uTs[0:16, hc * 512 + hsub * 128:
                                    hc * 512 + (hsub + 1) * 128],
                                P2g[0:16, ts0:ts0 + tw],
                                start=False, stop=True)
                        step("pe", f_hp, waits=wts, tag=f"pe_hp_{hc}_{kk}")

                        step("sc",
                             lambda e, hc=hc, hsub=hsub, tw=tw, apar=apar,
                             kk=kk: e.activation(
                                 rbuf[:, kk % 2, 0:tw], psGa[apar][:, 0:tw],
                                 AF.Relu, scale=sc1c[:, hc * 4 + hsub:
                                                     hc * 4 + hsub + 1]),
                             waits=[f"pe_hp_{hc}_{kk}"] +
                             use(f"rbuf{kk % 2}"), tag=f"sc_r_{hc}_{kk}")
                        done(f"psGa{apar}", f"sc_r_{hc}_{kk}")

                        step("ve",
                             lambda e, hsub=hsub, ts0=ts0, tw=tw, kk=kk:
                             e.tensor_tensor(
                                 r2slab[:, hsub, ts0:ts0 + tw],
                                 rbuf[:, kk % 2, 0:tw], rbuf[:, kk % 2, 0:tw],
                                 op=ALU.mult),
                             waits=[f"sc_r_{hc}_{kk}"], tag=f"ve_r2_{hc}_{kk}")
                        done(f"rbuf{kk % 2}", f"ve_r2_{hc}_{kk}")
                        kk += 1

                dk = 0
                for oc in range(8):
                    for t5, (ts0, tw) in enumerate(TSL):
                        bpar = dk % 2
                        wts = use(f"psGb{bpar}")
                        if dk == 0:
                            wts += [f"ve_r2_{hc}_{kk-1}", f"ve_t2_{hc}_1",
                                    "ve_c2_7"]

                        def f_dn2(e, oc=oc, ts0=ts0, tw=tw, bpar=bpar):
                            r = None
                            for hsub in range(4):
                                r = e.matmul(
                                    psGb[bpar][:, 0:tw],
                                    tern2b[:, hsub, oc * 128:(oc + 1) * 128],
                                    r2slab[:, hsub, ts0:ts0 + tw],
                                    start=(hsub == 0), stop=(hsub == 3))
                            return r
                        step("pe", f_dn2, waits=wts, tag=f"pe_d2_{hc}_{dk}")

                        step("ve",
                             lambda e, oc=oc, ts0=ts0, tw=tw, bpar=bpar:
                             e.scalar_tensor_tensor(
                                 g.xT[:, oc, ts0:ts0 + tw],
                                 psGb[bpar][:, 0:tw], sc2c[:, oc:oc + 1],
                                 g.xT[:, oc, ts0:ts0 + tw],
                                 op0=ALU.mult, op1=ALU.add),
                             waits=[f"pe_d2_{hc}_{dk}"],
                             tag=f"ve_d2_{hc}_{dk}")
                        done(f"psGb{bpar}", f"ve_d2_{hc}_{dk}")
                        dk += 1
                done("ternb", f"pe_hp_{hc}_{kk-1}")
                done("tern2b", f"pe_d2_{hc}_{dk-1}")
                done("r2_pe", f"pe_d2_{hc}_{dk-1}")

    barrier()

    with ExitStack() as ph2:
        (metc, sig, unp1, mh1, h2s, db1x, dbasex, hUb, hVb, encvb, dech,
         Vtok, qT2, qm2, qq2, phic, rowbuf, gcol, gbt, Qb, ctxb, OSh,
         wideD, uw, abt, cpt0, cptA, cptB, tauf, Mh, maskm, maska) = alloc(
            ph2,
            ("sb", "metc", [128, 8], F32),
            ("sb", "sig", [128, 16], F32),
            ("sb", "unp1", [128, 64], F32),
            ("sb", "mh1", [128, 4], F32),
            ("sb", "h2s", [128, 16], BF16),
            ("sb", "db1x", [128, 4], F32),
            ("sb", "dbasex", [128, 8], F32),
            ("sb", "hUb", [32, 8, 128], BF16),
            ("sb", "hVb", [32, 8, 128], BF16),
            ("sb", "encvb", [128, 8, 128], BF16),
            ("sb", "dech", [128, 1024], BF16),
            ("sb", "Vtok", [128, 17, 128], BF16),
            ("sb", "qT2", [128, TEXT], F32),
            ("sb", "qm2", [128, TEXT], F32),
            ("sb", "qq2", [128, TEXT], F32),
            ("sb", "phic", [128, 17], F32),
            ("sb", "rowbuf", [1, TEXT], F32),
            ("sb", "gcol", [128, 17], F32),
            ("sb", "gbt", [128, 512], F32),
            ("sb", "Qb", [128, TEXT], BF16),
            ("sb", "ctxb", [128, TEXT], F32),
            ("sb", "OSh", [128, TEXT], BF16),
            ("sb", "wideD", [128, 8, 256], F32),
            ("sb", "uw", [128, 8, 256], F32),
            ("sb", "abt", [128, 256], BF16),
            ("sb", "cpt0", [128, 8, 64], F32),
            ("sb", "cptA", [128, 8, 64], F32),
            ("sb", "cptB", [128, 8, 64], F32),
            ("sb", "tauf", [128, 3, 8], F32),
            ("sb", "Mh", [128, 128], BF16),
            ("sb", "maskm", [128, 256], BF16),
            ("sb", "maska", [128, 256], BF16))
        psH2 = [alloc(ph2, ("ps", f"psH2{j}", [128, 512]))[0]
                for j in range(4)]

        dma("gp", maskm[:], g.dmaskm[:], tag="ld_mm")
        dma("gp", maska[:], g.dmaska[:], tag="ld_ma")
        dma("sy", db1x[:], g.db1[:], tag="ld_db1")
        dma("sy", dbasex[:], g.dbase[:], tag="ld_base")
        dma("gp", hUb[:].rearrange("p a b -> p (a b)"), g.dhU[:],
            tag="ld_hU")
        dma("gp", hVb[:].rearrange("p a b -> p (a b)"), g.dhV[:],
            tag="ld_hV")
        dma("gp", unp1[:], g.dcout1[:], waits=["cc_x1_gp"], tag="ld_unp1")

        with ExitStack() as pmw:
            (mw1b, mw2b) = alloc(pmw, ("sb", "mw1b", [128, 4, 512], BF16),
                                 ("sb", "mw2b", [128, 2, 1024], BF16))

            def f_slot1(e, F):
                r = None
                for sidx in range(4):
                    if sidx == 0:
                        r = e.tensor_scalar(unp1[:, 0:16], unp1[:, 0:16],
                                            g.indc[:, 0:1], None,
                                            op0=ALU.mult)
                    else:
                        r = e.scalar_tensor_tensor(
                            unp1[:, 0:16],
                            unp1[:, sidx * 16:(sidx + 1) * 16],
                            g.indc[:, sidx:sidx + 1], unp1[:, 0:16],
                            op0=ALU.mult, op1=ALU.add)
                    if sidx < 3:
                        F(r)
                return r
            step("ve", f_slot1, waits=["ld_unp1", "consts"], tag="ve_slot1")

            def f_sigc(e, F):
                F(e.tensor_scalar(sig[:, 0:8], unp1[:, 0:8], 1.0 / 4096.0,
                                  None, op0=ALU.mult))
                F(e.scalar_tensor_tensor(sig[:, 0:8], unp1[:, 8:16],
                                         0.5 / 512.0, sig[:, 0:8],
                                         op0=ALU.mult, op1=ALU.add))
                return e.tensor_copy(h2s[:, 8:16], sig[:, 0:8])
            step("ve", f_sigc, waits=["ve_slot1"], tag="ve_sigc")

            for mh in range(2):
                dma("gp", mw1b[:],
                    g.dw1[:].rearrange("p (a b) -> p a b", b=512)[
                        :, mh * 4:(mh + 1) * 4, :],
                    waits=(use("mw1b") if mh else []), tag=f"ld_mw1_{mh}")
                for j1 in range(4):
                    def f_h1(e, j1=j1, mh=mh):
                        r = None
                        for dc in range(4):
                            dcg = mh * 4 + dc
                            r = e.matmul(psH2[0][:, j1:j1 + 1],
                                         mw1b[:, dc, j1 * 128:(j1 + 1) * 128],
                                         h2s[:, 8 + dcg:9 + dcg],
                                         start=(mh == 0 and j1 == 0 and
                                                dc == 0),
                                         stop=(mh == 1 and j1 == 3 and
                                               dc == 3),
                                         skip_group_check=True)
                        return r
                    step("pe", f_h1,
                         waits=(["ve_sigc", f"ld_mw1_{mh}"]
                                if j1 == 0 else []),
                         tag=f"pe_h1_{mh}_{j1}")
                done("mw1b", f"pe_h1_{mh}_3")

            def f_h1a(e, F):
                r = None
                for j1 in range(4):
                    r = e.activation(mh1[:, j1:j1 + 1], psH2[0][:, j1:j1 + 1],
                                     AF.Gelu, bias=db1x[:, j1:j1 + 1])
                    if j1 < 3:
                        F(r)
                return r
            step("sc", f_h1a, waits=["pe_h1_1_3", "ld_db1"], tag="sc_h1")
            done("psH20", "sc_h1")
            step("ve", lambda e: e.tensor_copy(h2s[:, 0:4], mh1[:]),
                 waits=["sc_h1"], tag="ve_h1c")
            for mh in range(2):
                dma("gp", mw2b[:],
                    g.dw2[:].rearrange("p (a b) -> p a b", b=1024)[
                        :, mh * 2:(mh + 1) * 2, :],
                    waits=(use("mw2b") if mh else []), tag=f"ld_mw2_{mh}")
                for j2 in range(8):
                    def f_mod(e, j2=j2, mh=mh):
                        r = None
                        for j1 in range(2):
                            j1g = mh * 2 + j1
                            r = e.matmul(psH2[1][:, j2:j2 + 1],
                                         mw2b[:, j1, j2 * 128:(j2 + 1) * 128],
                                         h2s[:, j1g:j1g + 1],
                                         start=(mh == 0 and j2 == 0 and
                                                j1 == 0),
                                         stop=(mh == 1 and j2 == 7 and
                                               j1 == 1),
                                         skip_group_check=True)
                        return r
                    step("pe", f_mod,
                         waits=(["ve_h1c", f"ld_mw2_{mh}"]
                                if j2 == 0 else []),
                         tag=f"pe_mod_{mh}_{j2}")
                done("mw2b", f"pe_mod_{mh}_7")

            def f_met(e, F):
                r = None
                for j2 in range(8):
                    F(e.activation(metc[:, j2:j2 + 1],
                                   psH2[1][:, j2:j2 + 1], AF.Exp, scale=0.1,
                                   bias=dbasex[:, j2:j2 + 1]))
                    r = e.activation(metc[:, j2:j2 + 1],
                                     metc[:, j2:j2 + 1], AF.Ln,
                                     bias=g.epsc[:, 2:3])
                    if j2 < 7:
                        F(r)
                return r
            step("sc", f_met, waits=["pe_mod_1_7", "ld_base"], tag="sc_met")
            done("psH21", "sc_met")

        HLF = [(0, list(range(1, 9))), (1, list(range(9, 17)))]
        for h in range(NH):
            dma("gp", encvb[:],
                g.dencv[:].rearrange("p (a b) -> p a b", b=1024)[
                    :, :, h * 128:(h + 1) * 128],
                waits=use("encvb"), tag=f"ld_ev_{h}", semname="s_wa")
            dma("gp", dech[:],
                g.ddec[:].rearrange("p (a b) -> p a b", b=1024)[
                    :, h, :] if False else
                g.ddec[:, h * 1024:(h + 1) * 1024],
                waits=use("dech"), tag=f"ld_dh_{h}", semname="s_wb")

            for bi in range(NCH):
                vslot = bi % 2
                wts = use(f"psH2{vslot}")
                if bi == 0:
                    wts += [f"ld_ev_{h}"] + use("Vtok_pe")

                def f_v(e, h=h, bi=bi, vslot=vslot):
                    r = None
                    for dc in range(8):
                        r = e.matmul(psH2[vslot][:, 0:128],
                                     g.xnT[:, dc, bi * 128:(bi + 1) * 128],
                                     encvb[:, dc], start=(dc == 0),
                                     stop=(dc == 7))
                    return r
                step("pe", f_v, waits=wts, tag=f"pe_v_{h}_{bi}")
                step("sc",
                     lambda e, bi=bi, vslot=vslot: e.activation(
                         Vtok[:, bi], psH2[vslot][:, 0:128], AF.Relu),
                     waits=[f"pe_v_{h}_{bi}"], tag=f"sc_v_{h}_{bi}")
                done(f"psH2{vslot}", f"sc_v_{h}_{bi}")
            done("encvb", f"pe_v_{h}_{NCH-1}")

            dma("sc", qT2[:], g.dqspill[h], waits=use("qT2"),
                tag=f"ld_q_{h}", semname="s_qsp")

            def f_qm(e, F, h=h):
                F(e.tensor_scalar(qm2[:], qT2[:], metc[:, h:h + 1], -2.0,
                                  op0=ALU.mult, op1=ALU.mult))
                F(e.tensor_tensor(qq2[:], qT2[:], qm2[:], op=ALU.mult))
                F(e.tensor_scalar(qq2[:], qq2[:], -0.5, None, op0=ALU.mult))
                return e.tensor_copy(Qb[:], qT2[:])
            step("ve", f_qm, waits=[f"ld_q_{h}", "sc_met"] +
                 use("qm2", "Qb"), tag=f"ve_qm_{h}")

            for t5, (ts0, tw) in enumerate(TSL):
                step("pe",
                     lambda e, ts0=ts0, tw=tw: e.matmul(
                         psH2[2][0:1, 0:tw], g.onescol[:],
                         qq2[:, ts0:ts0 + tw], start=True, stop=True),
                     waits=[f"ve_qm_{h}"] + use("psH22"),
                     tag=f"pe_phr_{h}_{t5}")
                step("ve",
                     lambda e, ts0=ts0, tw=tw: e.tensor_copy(
                         rowbuf[0:1, ts0:ts0 + tw], psH2[2][0:1, 0:tw]),
                     waits=[f"pe_phr_{h}_{t5}"] + use("rowbuf_g"),
                     tag=f"ve_phr_{h}_{t5}")
                done("psH22", f"ve_phr_{h}_{t5}")
            for bi in range(NCH):
                step("pe",
                     lambda e, bi=bi: e.matmul(
                         psH2[2][:, 0:1], qq2[:, bi * 128:(bi + 1) * 128],
                         g.onescol[:], start=True, stop=True),
                     waits=[f"ve_phr_{h}_4"] + use("psH22"),
                     tag=f"pe_phc_{h}_{bi}")
                step("ve",
                     lambda e, bi=bi: e.tensor_copy(phic[:, bi:bi + 1],
                                                    psH2[2][:, 0:1]),
                     waits=[f"pe_phc_{h}_{bi}"], tag=f"ve_phc_{h}_{bi}")
                done("psH22", f"ve_phc_{h}_{bi}")

            for hv, blist in HLF:
                for bb, bi in enumerate(blist):
                    for sd in range(2):
                        jc = bi - 1 + sd
                        pslot = (2 * bb + sd) % 2
                        wts = use(f"psH2{pslot}")
                        if bb == 0 and sd == 0:
                            wts += [f"ve_phc_{h}_{NCH-1}"]
                            wts += use("wideD_ve", "wideD_dma")

                        def f_g2w(e, bi=bi, jc=jc, pslot=pslot):
                            e.matmul(psH2[pslot][:, 0:128],
                                     g.onessq[0:1, :],
                                     rowbuf[0:1, jc * 128:(jc + 1) * 128],
                                     start=True, stop=False,
                                     skip_group_check=True)
                            return e.matmul(
                                psH2[pslot][:, 0:128],
                                qm2[:, bi * 128:(bi + 1) * 128],
                                qT2[:, jc * 128:(jc + 1) * 128],
                                start=False, stop=True,
                                skip_group_check=True)
                        step("pe", f_g2w, waits=wts,
                             tag=f"pe_w_{h}_{hv}_{bb}_{sd}")

                        def f_d(e, F, bi=bi, bb=bb, sd=sd, pslot=pslot):
                            dsl = wideD[:, bb, sd * 128:(sd + 1) * 128]
                            F(e.scalar_tensor_tensor(
                                dsl, psH2[pslot][:, 0:128],
                                phic[:, bi:bi + 1],
                                maskm[:, sd * 128:(sd + 1) * 128],
                                op0=ALU.add, op1=ALU.mult))
                            F(e.tensor_scalar(dsl, dsl, 0.0, None,
                                              op0=ALU.max))
                            return e.tensor_tensor(
                                dsl, dsl, maska[:, sd * 128:(sd + 1) * 128],
                                op=ALU.add)
                        step("ve", f_d,
                             waits=[f"pe_w_{h}_{hv}_{bb}_{sd}"] +
                             (["ld_mm", "ld_ma"] if h == 0 and hv == 0 and
                              bb == 0 and sd == 0 else []),
                             tag=f"ve_d_{h}_{hv}_{bb}_{sd}")
                        done(f"psH2{pslot}", f"ve_d_{h}_{hv}_{bb}_{sd}")

                step("sc",
                     lambda e: e.activation(
                         uw[:].rearrange("p a b -> p (a b)"),
                         wideD[:].rearrange("p a b -> p (a b)"), AF.Sqrt,
                         bias=g.epsc[:, 1:2]),
                     waits=[f"ve_d_{h}_{hv}_7_1"] + use("uw"),
                     tag=f"sc_u1_{h}_{hv}")
                step("sc",
                     lambda e: e.activation(
                         uw[:].rearrange("p a b -> p (a b)"),
                         uw[:].rearrange("p a b -> p (a b)"), AF.Exp,
                         scale=-1.0),
                     waits=[f"sc_u1_{h}_{hv}"], tag=f"sc_u2_{h}_{hv}")

                dma("sc", g.dskew[hv][:, :, 0:256], wideD[:],
                    waits=[f"ve_d_{h}_{hv}_7_1"] +
                    ([f"sk_r_{h-1}_{hv}"] if h >= 1 else []),
                    tag=f"sk_w_{h}_{hv}", semname=f"s_sk{hv}")
                done("wideD_dma", f"sk_w_{h}_{hv}")
                sksrc = g.dskew[hv][:, :, 0:64].copy()
                sksrc.ap[0] = [2057, 128]
                sksrc.offset = 65
                dma("sc", cpt0[:], sksrc,
                    waits=[f"sk_w_{h}_{hv}"] + use("cpt0"),
                    tag=f"sk_r_{h}_{hv}", semname=f"s_sk{hv}")

                def f_sort(e, F, h=h, hv=hv):
                    def views(t, k, j, dr):
                        m = max(k // (2 * j), 1)
                        blk = 4 * j * m if 4 * j * m <= 64 else 64
                        ap = t[:].rearrange("p a b -> p (a b)").rearrange(
                            "p (u w) -> p u w", w=blk)
                        off = dr * 2 * j * m
                        lo = ap[:, :, off:off + 2 * j * m].rearrange(
                            "p u (q r) -> p u q r", r=2 * j)
                        return (lo[:, :, :, 0:j], lo[:, :, :, j:2 * j])
                    cur = cpt0
                    nxt = cptA
                    r = None
                    k = 2
                    while k <= 64:
                        j = k // 2
                        while j >= 1:
                            ndr = 1 if k == 64 else 2
                            for dr in range(ndr):
                                a_lo, a_hi = views(cur, k, j, dr)
                                o_lo, o_hi = views(nxt, k, j, dr)
                                if dr == 0:
                                    F(e.tensor_tensor(o_lo, a_lo, a_hi,
                                                      op=ALU.min))
                                    r = e.tensor_tensor(o_hi, a_lo, a_hi,
                                                        op=ALU.max)
                                else:
                                    F(e.tensor_tensor(o_lo, a_lo, a_hi,
                                                      op=ALU.max))
                                    r = e.tensor_tensor(o_hi, a_lo, a_hi,
                                                        op=ALU.min)
                                F(r)
                            cur = nxt
                            nxt = cptB if cur is cptA else cptA
                            j //= 2
                        k *= 2
                    return e.tensor_copy(
                        tauf[:, 0], cur[:, :, 31:32].rearrange(
                            "p a o -> p (a o)"))
                step("ve", f_sort,
                     waits=[f"sk_r_{h}_{hv}"] + use("cptA", "cptB", "tauf"),
                     tag=f"ve_srt_{h}_{hv}")

                def f_cnt(e, F):
                    tb = tauf[:, 0].rearrange("p (a o) -> p a o", o=1
                                              ).broadcast_to((128, 8, 64))
                    F(e.tensor_tensor(cptA[:], cpt0[:], tb, op=ALU.is_lt))
                    F(e.tensor_reduce(tauf[:, 1].rearrange(
                        "p (a o) -> p a o", o=1), cptA[:], axis=AX.X,
                        op=ALU.add))
                    F(e.tensor_tensor(cptA[:], cpt0[:], tb,
                                      op=ALU.is_equal))
                    F(e.tensor_reduce(tauf[:, 2].rearrange(
                        "p (a o) -> p a o", o=1), cptA[:], axis=AX.X,
                        op=ALU.add))
                    F(e.tensor_scalar(tauf[:, 1], tauf[:, 1], -1.0, 32.0,
                                      op0=ALU.mult, op1=ALU.add))
                    F(e.reciprocal(tauf[:, 2], tauf[:, 2]))
                    return e.tensor_tensor(tauf[:, 1], tauf[:, 1],
                                           tauf[:, 2], op=ALU.mult)
                step("ve", f_cnt, waits=[f"ve_srt_{h}_{hv}"],
                     tag=f"ve_cnt_{h}_{hv}")
                done("cpt0", f"ve_cnt_{h}_{hv}")
                done("cptA", f"ve_cnt_{h}_{hv}")
                done("cptB", f"ve_cnt_{h}_{hv}")
                done("tauf", f"ve_cnt_{h}_{hv}")

                def f_sel(e, F, h=h, hv=hv):
                    tb = tauf[:, 0].rearrange("p (a o) -> p a o", o=1
                                              ).broadcast_to((128, 8, 256))
                    fb = tauf[:, 1].rearrange("p (a o) -> p a o", o=1
                                              ).broadcast_to((128, 8, 256))
                    w3 = wideD[:]
                    meq = qq2[:, 0:2048].rearrange("p (a b) -> p a b", a=8)
                    F(e.tensor_tensor(meq, w3, tb, op=ALU.is_equal))
                    F(e.tensor_tensor(meq, meq, fb, op=ALU.mult))
                    F(e.tensor_tensor(w3, w3, tb, op=ALU.is_lt))
                    F(e.tensor_tensor(w3, w3, meq, op=ALU.add))
                    F(e.tensor_tensor(w3, w3, uw[:], op=ALU.mult))
                    F(e.tensor_reduce(tauf[:, 2].rearrange(
                        "p (a o) -> p a o", o=1), w3, axis=AX.X, op=ALU.add))
                    F(e.tensor_scalar(tauf[:, 2], tauf[:, 2],
                                      g.epsc[:, 1:2], None, op0=ALU.add))
                    F(e.reciprocal(tauf[:, 2], tauf[:, 2]))
                    zb = tauf[:, 2].rearrange("p (a o) -> p a o", o=1
                                              ).broadcast_to((128, 8, 256))
                    return e.tensor_tensor(w3, w3, zb, op=ALU.mult)
                step("ve", f_sel,
                     waits=[f"ve_cnt_{h}_{hv}", f"sc_u2_{h}_{hv}",
                            f"ve_phc_{h}_{NCH-1}"],
                     tag=f"ve_sel_{h}_{hv}")
                done("uw", f"ve_sel_{h}_{hv}")
                done("wideD_ve", f"ve_sel_{h}_{hv}")

                for bb, bi in enumerate(blist):
                    for sd in range(2):
                        step("pe",
                             lambda e, bb=bb, sd=sd: e.matmul(
                                 psH2[2][:, 0:128],
                                 wideD[:, bb, sd * 128:(sd + 1) * 128],
                                 g.identf[:], start=True, stop=True),
                             waits=[f"ve_sel_{h}_{hv}"] + use("psH22"),
                             tag=f"pe_at_{h}_{hv}_{bb}_{sd}")
                        step("ve",
                             lambda e, sd=sd: e.tensor_copy(
                                 abt[:, sd * 128:(sd + 1) * 128],
                                 psH2[2][:, 0:128]),
                             waits=[f"pe_at_{h}_{hv}_{bb}_{sd}"] +
                             (use("abt") if sd == 0 else []),
                             tag=f"ve_at_{h}_{hv}_{bb}_{sd}")
                        done("psH22", f"ve_at_{h}_{hv}_{bb}_{sd}")

                    def f_cx(e, bi=bi, hv=hv, bb=bb):
                        e.matmul(psH2[3][:, 0:128], Vtok[:, bi - 1],
                                 abt[:, 0:128], start=True, stop=False)
                        return e.matmul(psH2[3][:, 0:128], Vtok[:, bi],
                                        abt[:, 128:256], start=False,
                                        stop=True)
                    step("pe", f_cx,
                         waits=[f"ve_at_{h}_{hv}_{bb}_1",
                                f"sc_v_{h}_{NCH-1}"] + use("psH23"),
                         tag=f"pe_cx_{h}_{hv}_{bb}")
                    done("abt", f"pe_cx_{h}_{hv}_{bb}")
                    step("ve",
                         lambda e, bi=bi: e.tensor_copy(
                             ctxb[:, bi * 128:(bi + 1) * 128],
                             psH2[3][:, 0:128]),
                         waits=[f"pe_cx_{h}_{hv}_{bb}"] +
                         (use("ctxb") if hv == 0 and bb == 0 else []),
                         tag=f"ve_cx_{h}_{hv}_{bb}")
                    done("psH23", f"ve_cx_{h}_{hv}_{bb}")
                done("wideD_ve", f"pe_at_{h}_{hv}_7_1")

            def f_qn(e, h=h):
                return e.tensor_tensor(qq2[:], qT2[:], qT2[:], op=ALU.mult)
            step("ve", f_qn, waits=[f"ve_sel_{h}_1"], tag=f"ve_qn_{h}")
            for bi in range(NCH):
                step("pe",
                     lambda e, bi=bi: e.matmul(
                         psH2[2][:, 0:1], qq2[:, bi * 128:(bi + 1) * 128],
                         g.onescol[:], start=True, stop=True),
                     waits=[f"ve_qn_{h}"] + use("psH22"),
                     tag=f"pe_qn_{h}_{bi}")
                step("ve",
                     lambda e, bi=bi: e.tensor_copy(gcol[:, bi:bi + 1],
                                                    psH2[2][:, 0:1]),
                     waits=[f"pe_qn_{h}_{bi}"], tag=f"ve_gc_{h}_{bi}")
                done("psH22", f"ve_gc_{h}_{bi}")
            step("sc", lambda e: e.activation(gcol[:], gcol[:], AF.Sqrt),
                 waits=[f"ve_gc_{h}_{NCH-1}"], tag=f"sc_gq_{h}")

            def f_g2(e, F):
                F(e.tensor_scalar(qq2[:, 0:17], gcol[:], 0.2, None,
                                  op0=ALU.is_gt))
                F(e.tensor_scalar(gcol[:], gcol[:], g.epsc[:, 5:6], None,
                                  op0=ALU.max))
                F(e.reciprocal(gcol[:], gcol[:]))
                F(e.tensor_tensor(gcol[:], gcol[:], qq2[:, 0:17],
                                  op=ALU.mult))
                return e.tensor_scalar(gcol[:], gcol[:], 0.1, None,
                                       op0=ALU.mult)
            step("ve", f_g2, waits=[f"sc_gq_{h}"], tag=f"ve_g2_{h}")

            for bi in range(NCH):
                step("pe",
                     lambda e, bi=bi: e.matmul(
                         psH2[2][0:1, 0:128], gcol[:, bi:bi + 1],
                         g.identf[:], start=True, stop=True),
                     waits=[f"ve_g2_{h}"] + use("psH22", "rowbuf_wD"),
                     tag=f"pe_gr_{h}_{bi}")
                step("ve",
                     lambda e, bi=bi: e.tensor_copy(
                         rowbuf[0:1, bi * 128:(bi + 1) * 128],
                         psH2[2][0:1, 0:128]),
                     waits=[f"pe_gr_{h}_{bi}"], tag=f"ve_gr_{h}_{bi}")
                done("psH22", f"ve_gr_{h}_{bi}")
            done("rowbuf_g", f"ve_gr_{h}_{NCH-1}")


            step("pe",
                 lambda e, h=h: e.matmul(psH2[3][:, 0:128], hUb[:, h],
                                         hVb[:, h], start=True, stop=True),
                 waits=use("psH23") + (["ld_hU", "ld_hV"] if h == 0 else []),
                 tag=f"pe_M_{h}")
            step("ve", lambda e: e.tensor_copy(Mh[:], psH2[3][:, 0:128]),
                 waits=[f"pe_M_{h}"] + use("Mh"), tag=f"ve_M_{h}")
            done("psH23", f"ve_M_{h}")

            for t5, (ts0, tw) in enumerate(TSL):
                step("pe",
                     lambda e, ts0=ts0, tw=tw: e.matmul(
                         psH2[2][:, 0:tw], g.onessq[0:1, :],
                         rowbuf[0:1, ts0:ts0 + tw], start=True, stop=True),
                     waits=[f"ve_gr_{h}_{NCH-1}"] + use("psH22"),
                     tag=f"pe_gb_{h}_{t5}")
                step("ve",
                     lambda e, tw=tw: e.tensor_copy(gbt[:, 0:tw],
                                                    psH2[2][:, 0:tw]),
                     waits=[f"pe_gb_{h}_{t5}"] + use("gbt"),
                     tag=f"ve_gb_{h}_{t5}")
                done("psH22", f"ve_gb_{h}_{t5}")
                step("pe",
                     lambda e, ts0=ts0, tw=tw: e.matmul(
                         psH2[3][:, 0:tw], Mh[:], Qb[:, ts0:ts0 + tw],
                         start=True, stop=True),
                     waits=([f"ve_M_{h}"] if t5 == 0 else []) +
                     use("psH23"), tag=f"pe_hb_{h}_{t5}")

                def f_hb(e, F, ts0=ts0, tw=tw):
                    F(e.tensor_tensor(qq2[:, ts0:ts0 + tw],
                                      psH2[3][:, 0:tw], gbt[:, 0:tw],
                                      op=ALU.mult))
                    return e.tensor_tensor(ctxb[:, ts0:ts0 + tw],
                                           ctxb[:, ts0:ts0 + tw],
                                           qq2[:, ts0:ts0 + tw], op=ALU.add)
                step("ve", f_hb,
                     waits=[f"pe_hb_{h}_{t5}", f"ve_gb_{h}_{t5}",
                            f"ve_cx_{h}_1_7"], tag=f"ve_hb_{h}_{t5}")
                done("psH23", f"ve_hb_{h}_{t5}")
                done("gbt", f"ve_hb_{h}_{t5}")
            done("Mh", f"pe_hb_{h}_4")

            step("ve",
                 lambda e, h=h: e.tensor_tensor(OSh[:], qT2[:], ctxb[:],
                                                op=ALU.mult),
                 waits=[f"ve_hb_{h}_4"] + use("OSh"), tag=f"ve_os_{h}")
            done("qT2", f"ve_os_{h}")
            done("qm2", f"ve_os_{h}")
            done("Qb", f"pe_hb_{h}_4")
            done("ctxb", f"ve_os_{h}")
            done("Vtok_pe", f"pe_cx_{h}_1_7")
            done("rowbuf_wD", f"pe_w_{h}_1_7_1")

            for oc in range(8):
                for t5, (ts0, tw) in enumerate(TSL):
                    dslot = (oc * 5 + t5) % 2
                    wts = use(f"psH2{dslot}")
                    if oc == 0 and t5 == 0:
                        wts += [f"ld_dh_{h}", f"ve_os_{h}"]
                    step("pe",
                         lambda e, oc=oc, ts0=ts0, tw=tw, dslot=dslot:
                         e.matmul(psH2[dslot][:, 0:tw],
                                  dech[:, oc * 128:(oc + 1) * 128],
                                  OSh[:, ts0:ts0 + tw], start=True,
                                  stop=True),
                         waits=wts, tag=f"pe_dc_{h}_{oc}_{t5}")
                    step("ve",
                         lambda e, oc=oc, ts0=ts0, tw=tw, dslot=dslot:
                         e.tensor_tensor(g.xT[:, oc, ts0:ts0 + tw],
                                         g.xT[:, oc, ts0:ts0 + tw],
                                         psH2[dslot][:, 0:tw], op=ALU.add),
                         waits=[f"pe_dc_{h}_{oc}_{t5}"],
                         tag=f"ve_dc_{h}_{oc}_{t5}")
                    done(f"psH2{dslot}", f"ve_dc_{h}_{oc}_{t5}")
            done("OSh", f"pe_dc_{h}_7_4")
            done("dech", f"pe_dc_{h}_7_4")

    barrier()

    with ExitStack() as po:
        (otok,) = alloc(po, ("sb", "otok", [128, 2, 1024], F32))
        psT = [alloc(po, ("ps", f"psV{j}", [128, 512]))[0]
               for j in range(4)]
        for i in range(1, NCH):
            for dc in range(8):
                k = (i - 1) * 8 + dc
                wts = []
                if k >= 4:
                    wts.append(f"ve_o_{k-4}")
                step("pe",
                     lambda e, i=i, dc=dc, k=k: e.matmul(
                         psT[k % 4][:, 0:128],
                         g.xT[:, dc, i * 128:(i + 1) * 128],
                         g.identf[:], start=True, stop=True),
                     waits=wts, tag=f"pe_o_{k}")
                evw = [f"pe_o_{k}"]
                if dc == 0 and i >= 3:
                    evw.append(f"st_{i-2}")
                step("ve",
                     lambda e, i=i, dc=dc, k=k: e.tensor_copy(
                         otok[:, i % 2, dc * 128:(dc + 1) * 128],
                         psT[k % 4][:, 0:128]),
                     waits=evw, tag=f"ve_o_{k}")
            dma("sy", g.dy[(i - 1) * 128:i * 128, :], otok[:, i % 2],
                tag=f"st_{i}", waits=[f"ve_o_{(i-1)*8+7}"],
                semname=f"s_st{i % 2}")

        @blk.sync
        def _(e):
            e.wait_ge(sems["s_st0"], val("s_st0"))
            e.wait_ge(sems["s_st1"], val("s_st1"))

        if g.ddbg:
            @blk.scalar
            def _(e):
                if "dbg_a" in g.ddbg:
                    e.dma_start(out=g.ddbg["dbg_a"][:],
                                in_=g.xT[:].rearrange("p a b -> p (a b)")
                                ).then_inc(sems["s_sp"], 16)
                    inc("s_sp", 16)
                e.wait_ge(sems["s_sp"], val("s_sp"))
    rest.close()


_NC_CACHE = {}


def _exp_layout_up(w):
    ne = w.shape[0]
    return np.ascontiguousarray(
        w.reshape(ne, 8, 128, EI).transpose(0, 2, 1, 3).reshape(ne, 128, 8 * EI))


def _exp_layout_dn(w):
    ne = w.shape[0]
    wp = np.zeros((ne, 768, 1024), np.float32)
    wp[:, 0:EI] = w
    return np.ascontiguousarray(
        wp.reshape(ne, 6, 128, 1024).transpose(0, 2, 1, 3).reshape(ne, 128, 6144))


def _stage_inputs(inputs):
    x = np.ascontiguousarray(np.asarray(inputs["L0_attn_x"], dtype=np.float32))
    gamma = np.asarray(inputs["L0_attn_gamma"], np.float32)
    wqkv = np.ascontiguousarray(np.asarray(inputs["L0_attn_w_qkv"], np.float32))
    temp = np.asarray(inputs["L0_attn_temperature"], np.float32).reshape(16)
    wout = np.ascontiguousarray(np.asarray(inputs["L0_attn_w_out"], np.float32))
    gate_w = np.asarray(inputs["L0_ffn_gate_w"], np.float32)
    gate_b = np.asarray(inputs["L0_ffn_gate_bias"], np.float32)
    Wg = np.asarray(inputs["L0_ffn_Wg"], np.float32)
    Wu = np.asarray(inputs["L0_ffn_Wu"], np.float32)
    Wd = np.asarray(inputs["L0_ffn_Wd"], np.float32)
    Sg = np.asarray(inputs["L0_ffn_Sg"], np.float32)
    Su = np.asarray(inputs["L0_ffn_Su"], np.float32)
    Sd = np.asarray(inputs["L0_ffn_Sd"], np.float32)
    encq = np.asarray(inputs["L1_attn_enc_q"], np.float32)
    encv = np.asarray(inputs["L1_attn_enc_v"], np.float32)
    mw1 = np.asarray(inputs["L1_attn_metric_w1"], np.float32)
    mb1 = np.asarray(inputs["L1_attn_metric_b1"], np.float32)
    mw2 = np.asarray(inputs["L1_attn_metric_w2"], np.float32)
    mb2 = np.asarray(inputs["L1_attn_metric_b2"], np.float32)
    mbase = np.asarray(inputs["L1_attn_base_metric"], np.float32)
    hU = np.asarray(inputs["L1_attn_hebb_U"], np.float32)
    hV = np.asarray(inputs["L1_attn_hebb_V"], np.float32)
    dec = np.asarray(inputs["L1_attn_decoder"], np.float32)
    fb = np.asarray(inputs["L1_ffn_fast_b"], np.float32)
    wfc = np.asarray(inputs["L1_ffn_w_fc"], np.float32)
    wpr = np.asarray(inputs["L1_ffn_w_proj"], np.float32)
    lu = np.asarray(inputs["L1_ffn_u"], np.float32)
    lv = np.asarray(inputs["L1_ffn_v"], np.float32)
    lgate = np.float32(np.asarray(inputs["L1_ffn_gate"]))

    import ml_dtypes
    tt = np.arange(128)[:, None]
    jj = np.arange(256)[None, :]
    inband = (jj >= tt + 65) & (jj <= tt + 128)
    shared = {
        "wqkv": wqkv,
        "wout": np.ascontiguousarray(
            wout.reshape(8, 128, 1024).transpose(1, 0, 2).reshape(128, 8192)),
        "gamma_col": np.ascontiguousarray(gamma.reshape(8, 128).T),
        "temp_col": np.ascontiguousarray(
            np.repeat(temp, 64).reshape(8, 128).T) if False else
            np.ascontiguousarray(temp.reshape(8, 2).repeat(64, axis=1)
                                 .reshape(8, 128).T),
        "gate_wT": np.ascontiguousarray(gate_w.T),
        "gate_bias_b": np.ascontiguousarray(np.tile(gate_b[None, :], (128, 1))),
        "Wg_all": _exp_layout_up(np.concatenate(
            [Wg.transpose(0, 2, 1), Sg[0:704].T[None], Sg[704:1408].T[None]], 0)),
        "Wu_all": _exp_layout_up(np.concatenate(
            [Wu.transpose(0, 2, 1), Su[0:704].T[None], Su[704:1408].T[None]], 0)),
        "Wd_all": _exp_layout_dn(np.concatenate(
            [Wd.transpose(0, 2, 1), Sd[:, 0:704].T[None],
             Sd[:, 704:1408].T[None]], 0)),
        "encq": np.ascontiguousarray(
            encq.transpose(1, 0, 2).reshape(8, 128, 1024)
            .transpose(1, 0, 2).reshape(128, 8192)),
        "encv": np.ascontiguousarray(
            encv.transpose(1, 0, 2).reshape(8, 128, 1024)
            .transpose(1, 0, 2).reshape(128, 8192)),
        "mw1": np.ascontiguousarray(
            mw1.reshape(8, 128, 512).transpose(1, 0, 2).reshape(128, 4096)),
        "mb1_col": np.ascontiguousarray(mb1.reshape(4, 128).T),
        "mw2": np.ascontiguousarray(
            mw2.reshape(4, 128, 1024).transpose(1, 0, 2).reshape(128, 4096)),
        "mb2_col": np.ascontiguousarray(mb2.reshape(8, 128).T),
        "base_col": np.ascontiguousarray(mbase.T),
        "hebbUT": np.ascontiguousarray(
            hU.transpose(0, 2, 1).transpose(1, 0, 2).reshape(32, 1024)),
        "hebbV": np.ascontiguousarray(
            hV.transpose(1, 0, 2).reshape(32, 1024)),
        "decoder": np.ascontiguousarray(
            dec.reshape(8, 128, 1024).transpose(1, 0, 2).reshape(128, 8192)),
        "w_fcT": np.ascontiguousarray(wfc.T),
        "w_projT": np.ascontiguousarray(wpr.T),
        "uT": np.ascontiguousarray(lu.T),
        "vT_lr": np.ascontiguousarray(
            lv.T.reshape(8, 128, 16).transpose(1, 0, 2).reshape(128, 128)),
        "fbT": np.ascontiguousarray(fb.T),
        "gate_col": np.full((16, 1), lgate, np.float32),
        "ident_bf": np.eye(128, dtype=ml_dtypes.bfloat16),
        "ident_f32": np.eye(128, dtype=np.float32),
        "ones_f32_sq": np.ones((128, 128), np.float32),
        "ones_f32_col": np.ones((128, 1), np.float32),
        "mask_add": np.where(inband, 0.0, BIG).astype(np.float32),
        "mask_mult": np.where(jj == tt + 128, 0.0, 1.0).astype(np.float32),
        "selrow": np.concatenate(
            [np.r_[np.ones(64, np.float32), np.zeros(64, np.float32)],
             np.r_[np.zeros(64, np.float32), np.ones(64, np.float32)]]
        )[None, :],
        "eps_col": np.tile(np.array(
            [1e-5, 1e-8, 1.0, 1e-20, 0.2, 1e-12, 0.0, 0.0],
            np.float32)[None, :], (128, 1)),
    }
    in_maps = []
    for c in range(8):
        b, half = c // 2, c % 2
        if half == 0:
            xs = np.concatenate(
                [np.repeat(x[b, 0:1], HALO, axis=0), x[b, 0:TOWN]], axis=0)
        else:
            xs = x[b, TOWN - HALO:T]
        ind = np.zeros((128, 4), np.float32)
        ind[:, b] = 1.0
        m = dict(shared)
        m["x"] = np.ascontiguousarray(xs)
        m["ind_col"] = ind
        in_maps.append(m)
    return in_maps


def kernel(**inputs):
    in_maps = _stage_inputs(inputs)
    key = ()
    if key not in _NC_CACHE:
        _NC_CACHE[key] = build_program()
    nc = _NC_CACHE[key]
    res = run_bass_kernel_spmd(nc, in_maps, core_ids=list(range(8)))
    out = np.zeros((B, T, D), np.float32)
    for c in range(8):
        b, half = c // 2, c % 2
        out[b, half * TOWN:(half + 1) * TOWN] = res.results[c]["y"]
    return out
